# version 06
# baseline (speedup 1.0000x reference)
import numpy as np
import ml_dtypes
from collections import defaultdict
from contextlib import ExitStack

import concourse.bass as bass
import concourse.mybir as mybir
from concourse.bass import IndirectOffsetOnAxis
from concourse.bass_utils import run_bass_kernel_spmd

F32 = mybir.dt.float32
BF16 = mybir.dt.bfloat16
U32 = mybir.dt.uint32
AF = mybir.ActivationFunctionType
ALU = mybir.AluOpType
AX = mybir.AxisListType

DM = 4096
NT = 1024
LSEQ = 4096
NCORES = 8
EPS = 1e-5
MAGIC = 1.5 * 2.0 ** 23
PIHALF = float(np.pi / 2)
TWO_PI = float(2 * np.pi)
INV_2PI = float(1.0 / (2 * np.pi))
INV_SQRT128 = float(1.0 / np.sqrt(128.0))

_f = np.arange(DM)
PI = (_f % 128) * 32 + (_f // 128)
PIINV = np.empty(DM, dtype=np.int64)
PIINV[PI] = _f

COMPUTE_STREAMS = ("pe", "act", "dve", "pool")
DMA_STREAMS = ("sp", "spd", "actd")

CONST_NAMES = ["eps", "one", "magic", "pihalf", "neg2pi", "inv2pi", "magicneg", "pi"]
CONST_PYVALS = [EPS, 1.0, MAGIC, PIHALF, -TWO_PI, INV_2PI, -MAGIC, float(np.pi)]
CONSTS_HOST = np.ascontiguousarray(
    np.tile(np.array(CONST_PYVALS, dtype=np.float32), (128, 1)))


class Prog:
    def __init__(self, nc):
        self.nc = nc
        self.last_write = {}
        self.readers = defaultdict(list)
        self.counts = defaultdict(int)
        self._records = []

    def _dep(self, stream, res, write):
        deps = []
        lw = self.last_write.get(res)
        if lw is not None:
            deps.append(lw)
        if write:
            deps.extend(self.readers.get(res, ()))
        return deps

    def op(self, stream, fn, r=(), w=()):
        deps = []
        for res in r:
            deps.extend(self._dep(stream, res, False))
        for res in w:
            deps.extend(self._dep(stream, res, True))
        deps = [(t, q) for (t, q) in deps
                if not (t == stream and q > self.counts[stream])]
        self.counts[stream] += 1
        seq = self.counts[stream]
        self._records.append((stream, fn, deps))
        for res in r:
            self.readers[res].append((stream, seq))
        for res in w:
            self.last_write[res] = (stream, seq)
            self.readers[res] = []
        return seq

    dma = op

    def barrier(self):
        snap = {t: self.counts[t] for t in self.counts if self.counts[t] > 0}
        streams = set(list(COMPUTE_STREAMS) + list(DMA_STREAMS))
        for st in streams:
            deps = [(t, q) for t, q in snap.items() if t != st]
            self._records.append((st, None, deps))

    def sync_point(self, stream, r=()):
        deps = []
        for res in r:
            deps.extend(self._dep(stream, res, False))
        self._records.append((stream, None, deps))

    def finalize(self, block, sems):
        dep_targets = defaultdict(set)
        for stream, fn, deps in self._records:
            for (t, q) in deps:
                dep_targets[t].add(q)
        targets = defaultdict(set)
        for s in dep_targets:
            targets[s] |= dep_targets[s]
        for s, n in self.counts.items():
            if n > 0:
                targets[s].add(n)
            if s in DMA_STREAMS:
                targets[s].update(range(1, n + 1))
        eng_section = {
            "pe": "tensor", "act": "scalar", "dve": "vector",
            "pool": "gpsimd", "sp": "sync", "spd": "gpsimd", "actd": "scalar",
        }
        by_section = defaultdict(list)
        seqno = defaultdict(int)
        for stream, fn, deps in self._records:
            if fn is not None:
                seqno[stream] += 1
            by_section[eng_section[stream]].append((stream, seqno[stream], fn, deps))
        observed = defaultdict(lambda: defaultdict(int))

        def make_body(section, items):
            def body(eng):
                pending = defaultdict(int)
                section_streams = {s for s, _, _, _ in items}
                for stream, seq, fn, deps in items:
                    need = defaultdict(int)
                    for (t, q) in deps:
                        need[t] = max(need[t], q)
                    for t in sorted(need):
                        q = need[t]
                        if observed[stream][t] >= q:
                            continue
                        mult = 16 if t in DMA_STREAMS else 1
                        eng.wait_ge(sems[t], q * mult)
                        for s2 in section_streams:
                            observed[s2][t] = max(observed[s2][t], q)
                    if fn is None:
                        continue
                    inst = fn(eng)
                    mult = 16 if stream in DMA_STREAMS else 1
                    pending[stream] += mult
                    if seq in targets[stream]:
                        inst.then_inc(sems[stream], pending[stream])
                        pending[stream] = 0
                    if stream in DMA_STREAMS and seq in dep_targets[stream]:
                        eng.wait_ge(sems[stream], seq * 16)
                        observed[stream][stream] = max(
                            observed[stream][stream], seq)
            return body

        for section in ("sync", "tensor", "scalar", "vector", "gpsimd"):
            if section in by_section:
                getattr(block, section)(make_body(section, by_section[section]))


class Ctx:
    def __init__(self):
        self.nc = bass.Bass("TRN2", target_bir_lowering=False)
        self.p = Prog(self.nc)
        self.stack = ExitStack()

    def _uniq(self, name):
        if not hasattr(self, "_names"):
            self._names = set()
        base, i = name, 1
        while name in self._names:
            i += 1
            name = f"{base}_{i}"
        self._names.add(name)
        return name

    def sbuf(self, name, shape, dtype):
        st = getattr(self, "cur_scope", None) or self.stack
        return st.enter_context(
            self.nc.sbuf_tensor(self._uniq(name), list(shape), dtype))

    def scope(self):
        from contextlib import ExitStack as _ES
        return _ES()

    def psum(self, name, shape, dtype=F32):
        return self.stack.enter_context(self.nc.psum_tensor(name, list(shape), dtype))

    def din(self, name, shape, dtype=F32):
        return self.nc.dram_tensor(name, list(shape), dtype, kind="ExternalInput").ap()

    def dout(self, name, shape, dtype=F32):
        return self.nc.dram_tensor(name, list(shape), dtype, kind="ExternalOutput").ap()

    def dram(self, name, shape, dtype=F32):
        return self.nc.dram_tensor(self._uniq(name), list(shape), dtype,
                                   kind="Internal").ap()

    def load_consts(self):
        self.consts = self.sbuf("consts_sb", [128, len(CONST_NAMES)], F32)
        cd = self.din("consts_in", [128, len(CONST_NAMES)])
        self.p.dma("sp", lambda e: e.dma_start(out=self.consts[:], in_=cd),
                   w=["consts"])
        for i, v in enumerate(CONST_PYVALS):
            self.nc.const_aps.register(F32, v, self.consts[:, i:i + 1])

    def ccol(self, name):
        i = CONST_NAMES.index(name)
        return self.consts[:, i:i + 1]

    def build(self, out_resources):
        for s in DMA_STREAMS:
            self.p.sync_point(s, r=out_resources)
        used = set()
        for stream, fn, deps in self.p._records:
            if fn is not None:
                used.add(stream)
            for (t, _) in deps:
                used.add(t)
        sems = {}
        for s in sorted(used):
            sems[s] = self.stack.enter_context(self.nc.semaphore("sem_" + s))
        block = self.stack.enter_context(self.nc.Block())
        self.p.finalize(block, sems)
        self.stack.close()
        return self.nc



def fm_stage_ln(ctx, tag, TH, nth, xout, emit_pre, mean_ps=None, e2_ps=None,
                mres=None, eres=None, bufs=None):
    nc, p = ctx.nc, ctx.p
    assert nth == 2
    ones_bf = ctx.sbuf(tag + "_ones", [128, 128], BF16)
    p.op("dve", lambda e: e.memset(ones_bf[:], 1.0), w=[tag + "ones"])
    if bufs is None:
        pre_sb = ctx.sbuf(tag + "_pre", [128, 2 * TH], F32)
        pre_bf = ctx.sbuf(tag + "_prebf", [128, 2 * TH], BF16)
        sq_bf = ctx.sbuf(tag + "_sqbf", [128, 2 * TH], BF16)
        mu_b = ctx.sbuf(tag + "_mu", [128, 2 * TH], F32)
        e2_b = ctx.sbuf(tag + "_e2", [128, 2 * TH], F32)
        rstd_b = ctx.sbuf(tag + "_rstd", [128, 2 * TH], F32)
    else:
        pre_sb, pre_bf, sq_bf, mu_b, e2_b, rstd_b = bufs
    spill = ctx.dram(tag + "_spill", [32 * 128, TH * nth], F32)
    if mean_ps is None:
        mean_ps = ctx.psum(tag + "_mps", [128, 2 * TH], F32)
        e2_ps = ctx.psum(tag + "_eps", [128, 2 * TH], F32)
    mres = mres or tag + "mps"
    eres = eres or tag + "eps" 

    for oc in range(32):
        for th in range(nth):
            bi = th
            dst = pre_sb[:, bi * TH:(bi + 1) * TH]
            emit_pre(oc, th, dst, tag + f"pre{bi}")
            p.op("dve", lambda e, dst=dst, bi=bi: e.tensor_copy(
                pre_bf[:, bi * TH:(bi + 1) * TH], dst),
                r=[tag + f"pre{bi}"], w=[tag + f"prebf{bi}"])
            p.op("act", lambda e, bi=bi: e.activation(
                sq_bf[:, bi * TH:(bi + 1) * TH],
                pre_bf[:, bi * TH:(bi + 1) * TH], AF.Square),
                r=[tag + f"prebf{bi}"], w=[tag + f"sqbf{bi}"])
            p.op("pe", lambda e, bi=bi, oc=oc: e.matmul(
                mean_ps[:, bi * TH:(bi + 1) * TH], ones_bf[:],
                pre_bf[:, bi * TH:(bi + 1) * TH],
                start=(oc == 0), stop=(oc == 31)),
                r=[tag + f"prebf{bi}", tag + "ones"], w=[mres])
            p.op("pe", lambda e, bi=bi, oc=oc: e.matmul(
                e2_ps[:, bi * TH:(bi + 1) * TH], ones_bf[:],
                sq_bf[:, bi * TH:(bi + 1) * TH],
                start=(oc == 0), stop=(oc == 31)),
                r=[tag + f"sqbf{bi}", tag + "ones"], w=[eres])
            p.dma("sp", lambda e, dst=dst, oc=oc, th=th: e.dma_start(
                out=spill[oc * 128:(oc + 1) * 128, th * TH:(th + 1) * TH],
                in_=dst),
                r=[tag + f"pre{bi}"], w=[tag + "spill"])
    p.op("act", lambda e: e.activation(mu_b[:], mean_ps[:], AF.Copy,
                                       scale=1.0 / DM),
         r=[mres], w=[tag + "mu"])
    p.op("act", lambda e: e.activation(e2_b[:], e2_ps[:], AF.Copy,
                                       scale=1.0 / DM),
         r=[eres], w=[tag + "e2"])
    p.op("dve", lambda e: e.tensor_mul(rstd_b[:], mu_b[:], mu_b[:]),
         r=[tag + "mu"], w=[tag + "rstd"])
    p.op("dve", lambda e: e.tensor_sub(e2_b[:], e2_b[:], rstd_b[:]),
         r=[tag + "rstd", tag + "e2"], w=[tag + "e2"])
    p.op("act", lambda e: e.activation(e2_b[:], e2_b[:], AF.Ln, bias=EPS),
         r=[tag + "e2", "consts"], w=[tag + "e2"])
    p.op("act", lambda e: e.activation(rstd_b[:], e2_b[:], AF.Exp, scale=-0.5),
         r=[tag + "e2"], w=[tag + "rstd"])
    for oc in range(32):
        for th in range(nth):
            bi = th
            dst = pre_sb[:, bi * TH:(bi + 1) * TH]
            p.dma("sp", lambda e, dst=dst, oc=oc, th=th: e.dma_start(
                out=dst,
                in_=spill[oc * 128:(oc + 1) * 128, th * TH:(th + 1) * TH]),
                r=[tag + "spill", tag + f"pre{bi}", tag + f"prebf{bi}"],
                w=[tag + f"pre{bi}"])
            p.op("dve", lambda e, dst=dst, bi=bi: e.tensor_sub(
                dst, dst, mu_b[:, bi * TH:(bi + 1) * TH]),
                r=[tag + f"pre{bi}", tag + "mu"], w=[tag + f"pre{bi}"])
            p.op("dve", lambda e, dst=dst, bi=bi: e.tensor_mul(
                dst, dst, rstd_b[:, bi * TH:(bi + 1) * TH]),
                r=[tag + f"pre{bi}", tag + "rstd"], w=[tag + f"pre{bi}"])
            p.dma("sp", lambda e, dst=dst, oc=oc, th=th: e.dma_start(
                out=xout[oc * 128:(oc + 1) * 128, th * TH:(th + 1) * TH],
                in_=dst),
                r=[tag + f"pre{bi}"], w=["xout"])



def emit_B(ctx, pool, xin, xout, ident, dbg=False):
    nc, p = ctx.nc, ctx.p
    wb = ctx.din("wb", [32, 128, 32, 128])
    gw = ctx.din("gw", [128, 32, 8])
    aT = ctx.din("aT", [128, 32, 64])
    ber = ctx.din("ber", [64, DM])
    r64 = ctx.din("r64", [8, 64])

    xb = ctx.sbuf("xb", [128, 32 * NT], BF16)
    for ic in range(32):
        p.dma("spd", lambda e, ic=ic: e.dma_start(
            out=xb[:, ic * NT:(ic + 1) * NT], in_=xin[ic * 128:(ic + 1) * 128, :]),
            w=["xb"])

    gwt = ctx.sbuf("gwt", [128, 32 * 8], BF16)
    p.dma("spd", lambda e: e.dma_start(out=gwt[:], in_=gw.rearrange("p a b -> p (a b)")),
          w=["gwt"])
    ps_big = pool[:, 0:NT]
    g_ps = ps_big[0:8, :]
    for ic in range(32):
        p.op("pe", lambda e, ic=ic: e.matmul(
            g_ps[:, 0:512], gwt[:, ic * 8:(ic + 1) * 8],
            xb[:, ic * NT:ic * NT + 512], start=(ic == 0), stop=(ic == 31)),
            r=["gwt", "xb"], w=["PBA"])
        p.op("pe", lambda e, ic=ic: e.matmul(
            g_ps[:, 512:1024], gwt[:, ic * 8:(ic + 1) * 8],
            xb[:, ic * NT + 512:ic * NT + 1024], start=(ic == 0), stop=(ic == 31)),
            r=["gwt", "xb"], w=["PBB"])
    g_sb = ctx.sbuf("g_sb", [8, NT], F32)
    p.op("act", lambda e: e.activation(g_sb[:, 0:512], g_ps[:, 0:512], AF.Copy),
         r=["PBA"], w=["g_sb"])
    p.op("act", lambda e: e.activation(g_sb[:, 512:1024], g_ps[:, 512:1024], AF.Copy),
         r=["PBB"], w=["g_sb"])


    gt_ps = pool[:, 1024:1152]
    g_tm = ctx.sbuf("g_tm", [128, 8 * 8], F32)
    for tt in range(8):
        p.op("pe", lambda e, tt=tt: e.transpose(
            gt_ps[:, 0:8], g_sb[:, tt * 128:(tt + 1) * 128], ident[0:8, 0:8]),
            r=["g_sb", "ident"], w=["gt_ps"])
        p.op("act", lambda e, tt=tt: e.activation(
            g_tm[:, tt * 8:(tt + 1) * 8], gt_ps[:, 0:8], AF.Copy),
            r=["gt_ps"], w=["g_tm"])

    m1 = ctx.sbuf("m1", [128, 8], F32)
    m2 = ctx.sbuf("m2", [128, 8], F32)
    w2e = ctx.sbuf("w2e", [128, 8], F32)
    winv = ctx.sbuf("winv", [128, 8], F32)
    gtmp = ctx.sbuf("gtmp", [128, 8 * 8], F32)
    eq1 = ctx.sbuf("eq1", [128, 8 * 8], F32)
    wd = ctx.sbuf("wd", [128, 8 * 8], F32)
    for tt in range(8):
        sl = slice(tt * 8, (tt + 1) * 8)
        gtile = g_tm[:, sl]
        p.op("dve", lambda e, gtile=gtile, tt=tt: e.reduce_max(
            m1[:, tt:tt + 1], gtile, AX.X), r=["g_tm"], w=["m1"])
        p.op("dve", lambda e, gtile=gtile, tt=tt, sl=sl: e.tensor_scalar(
            eq1[:, sl], gtile, m1[:, tt:tt + 1], None, ALU.is_equal),
            r=["g_tm", "m1"], w=["eq1"])
        p.op("dve", lambda e, gtile=gtile, tt=tt, sl=sl: e.scalar_tensor_tensor(
            gtmp[:, sl], eq1[:, sl], -1e30, gtile, ALU.mult, ALU.add),
            r=["g_tm", "eq1"], w=["gtmp"])
        p.op("dve", lambda e, tt=tt, sl=sl: e.reduce_max(
            m2[:, tt:tt + 1], gtmp[:, sl], AX.X), r=["gtmp"], w=["m2"])
    p.op("dve", lambda e: e.tensor_sub(w2e[:], m2[:], m1[:]),
         r=["m1", "m2"], w=["w2e"])
    p.op("act", lambda e: e.activation(w2e[:], w2e[:], AF.Exp),
         r=["w2e"], w=["w2e"])
    p.op("act", lambda e: e.activation(winv[:], w2e[:], AF.Copy, bias=1.0),
         r=["w2e", "consts"], w=["winv"])
    p.op("dve", lambda e: e.reciprocal(winv[:], winv[:]), r=["winv"], w=["winv"])
    p.op("act", lambda e: e.activation(winv[:], winv[:], AF.Copy, scale=2.0),
         r=["winv"], w=["winv"])
    p.op("dve", lambda e: e.tensor_mul(w2e[:], w2e[:], winv[:]),
         r=["w2e", "winv"], w=["w2e"])
    for tt in range(8):
        sl = slice(tt * 8, (tt + 1) * 8)
        gtile = g_tm[:, sl]
        p.op("dve", lambda e, gtile=gtile, tt=tt, sl=sl: e.tensor_scalar(
            gtmp[:, sl], gtile, m2[:, tt:tt + 1], None, ALU.is_equal),
            r=["g_tm", "m2"], w=["gtmp"])
        p.op("dve", lambda e, tt=tt, sl=sl: e.tensor_scalar(
            wd[:, sl], eq1[:, sl], winv[:, tt:tt + 1], None, ALU.mult),
            r=["eq1", "winv"], w=["wd"])
        p.op("dve", lambda e, tt=tt, sl=sl: e.scalar_tensor_tensor(
            wd[:, sl], gtmp[:, sl], w2e[:, tt:tt + 1], wd[:, sl],
            ALU.mult, ALU.add), r=["gtmp", "w2e", "wd"], w=["wd"])

    wdf = ctx.sbuf("wdf", [8, NT], F32)
    for tt in range(8):
        p.op("pe", lambda e, tt=tt: e.transpose(
            gt_ps[0:8, 0:128], wd[:, tt * 8:(tt + 1) * 8], ident[:]),
            r=["wd", "ident"], w=["gt_ps"])
        p.op("act", lambda e, tt=tt: e.activation(
            wdf[:, tt * 128:(tt + 1) * 128], gt_ps[0:8, 0:128], AF.Copy),
            r=["gt_ps"], w=["wdf"])
    wdf_bf = ctx.sbuf("wdf_bf", [8, NT], BF16)
    p.op("dve", lambda e: e.tensor_copy(wdf_bf[:], wdf[:]), r=["wdf"], w=["wdf_bf"])
    r64_bf = ctx.sbuf("r64_bf", [8, 64], BF16)
    p.dma("spd", lambda e: e.dma_start(out=r64_bf[:], in_=r64), w=["r64_bf"])
    at_sb = ctx.sbuf("at_sb", [128, 32 * 64], BF16)
    p.dma("spd", lambda e: e.dma_start(out=at_sb[:], in_=aT.rearrange("p a b -> p (a b)")),
          w=["at_sb"])
    u_ps = ps_big[0:64, :]
    for ic in range(32):
        for h in range(2):
            p.op("pe", lambda e, ic=ic, h=h: e.matmul(
                u_ps[:, h * 512:(h + 1) * 512], at_sb[:, ic * 64:(ic + 1) * 64],
                xb[:, ic * NT + h * 512:ic * NT + (h + 1) * 512],
                start=(ic == 0), stop=(ic == 31)),
                r=["at_sb", "xb"], w=["PBA" if h == 0 else "PBB"])
    u_sb = ctx.sbuf("u_sb", [64, NT], F32)
    p.op("act", lambda e: e.activation(u_sb[:], u_ps[:], AF.Copy),
         r=["PBA", "PBB"], w=["u_sb"])
    wrep_ps = ps_big[0:64, :]
    for h in range(2):
        p.op("pe", lambda e, h=h: e.matmul(
            wrep_ps[:, h * 512:(h + 1) * 512], r64_bf[:],
            wdf_bf[:, h * 512:(h + 1) * 512], start=True, stop=True),
            r=["wdf_bf", "r64_bf", "u_sb"], w=["PBA" if h == 0 else "PBB"])
    up_bf = ctx.sbuf("up_bf", [64, NT], BF16)
    p.op("dve", lambda e: e.tensor_mul(up_bf[:], u_sb[:], wrep_ps[:]),
         r=["u_sb", "PBA", "PBB"], w=["up_bf"])

    ber_sb = ctx.sbuf("ber_sb", [64, DM], BF16)
    p.dma("spd", lambda e: e.dma_start(out=ber_sb[:], in_=ber), w=["ber_sb"])

    wtile = ctx.sbuf("wtile", [128, 2 * 4096], BF16)
    out_ps = ps_big
    xin_t = ctx.sbuf("xin_t", [128, 2 * 512], F32)

    def emit_pre(oc, th, dst, dres):
        bi = oc % 2
        wsl = wtile[:, bi * 4096:(bi + 1) * 4096]
        if th == 0:
            p.dma("spd", lambda e, oc=oc, bi=bi: e.dma_start(
                out=wtile[:, bi * 4096:(bi + 1) * 4096],
                in_=wb[oc].rearrange("p a b -> p (a b)")),
                w=[f"wtile{bi}"])
        psl = out_ps[:, th * 512:(th + 1) * 512]
        for ic in range(32):
            p.op("pe", lambda e, ic=ic, oc=oc, th=th, psl=psl, wsl=wsl: e.matmul(
                psl, wsl[:, ic * 128:(ic + 1) * 128],
                xb[:, ic * NT + th * 512:ic * NT + (th + 1) * 512],
                start=(ic == 0), stop=False),
                r=[f"wtile{bi}", "xb"], w=["PBA" if th == 0 else "PBB"])
        p.op("pe", lambda e, oc=oc, th=th, psl=psl: e.matmul(
            psl, ber_sb[:, oc * 128:(oc + 1) * 128],
            up_bf[:, th * 512:(th + 1) * 512], start=False, stop=True),
            r=["ber_sb", "up_bf"], w=["PBA" if th == 0 else "PBB"])
        xsl = xin_t[:, th * 512:(th + 1) * 512]
        p.dma("sp", lambda e, oc=oc, th=th, xsl=xsl: e.dma_start(
            out=xsl, in_=xin[oc * 128:(oc + 1) * 128, th * 512:(th + 1) * 512]),
            w=[f"xin_t{th}"])
        p.op("dve", lambda e, psl=psl, xsl=xsl, dst=dst: e.scalar_tensor_tensor(
            dst, psl, 1.0, xsl, ALU.mult, ALU.add),
            r=["PBA" if th == 0 else "PBB", f"xin_t{th}"], w=[dres])

    fm_stage_ln(ctx, "bln", 512, 2, xout, emit_pre,
                mean_ps=pool[:, 1536:2560], e2_ps=pool[:, 2560:3584],
                mres="BSM", eres="BSE")


def build_B():
    ctx = Ctx()
    ctx.load_consts()
    identd = ctx.din("identd", [128, 128], F32)
    ident = ctx.sbuf("identB", [128, 128], F32)
    ctx.p.dma("spd", lambda e: e.dma_start(out=ident[:], in_=identd), w=["ident"])
    pool = ctx.psum("pool", [128, 3584], F32)
    xin = ctx.din("xin", [DM, NT])
    xout = ctx.dout("xout", [DM, NT])
    emit_B(ctx, pool, xin, xout, ident)
    return ctx.build(["xout"])



def _pack_inchunks(w):
    X = w.shape[1]
    return np.ascontiguousarray(w.reshape(32, 128, X).transpose(1, 0, 2))


def host_prep_B(inp):
    WbT = inp["L0_ffn_W_base"].T[PI][:, PI]
    wb = np.ascontiguousarray(
        WbT.reshape(32, 128, 32, 128).transpose(2, 1, 0, 3))
    gw = _pack_inchunks(inp["L0_ffn_gate_W"].T[PI])
    aT = _pack_inchunks(inp["L0_ffn_A"].reshape(64, DM).T[PI])
    ber = np.ascontiguousarray(
        inp["L0_ffn_B"].transpose(0, 2, 1).reshape(64, DM)[:, PI])
    r64 = np.repeat(np.eye(8, dtype=np.float32), 8, axis=1)
    return {"wb": wb, "gw": gw, "aT": aT, "ber": ber, "r64": r64,
            "identd": np.eye(128, dtype=np.float32),
            "consts_in": CONSTS_HOST}



def emit_D(ctx, pool, xin, out_tm, ident=None, xtm=None, mode="full"):
    nc, p = ctx.nc, ctx.p
    c1p = ctx.din("c1p", [128, 32, 16])
    c2T = ctx.din("c2T", [16, 14336])
    w1p = ctx.din("w1p", [14336, DM])
    w2p = ctx.din("w2p", [14336, DM])

    c1sb = ctx.sbuf("c1sb", [128, 32 * 16], BF16)
    p.dma("spd", lambda e: e.dma_start(out=c1sb[:], in_=c1p.rearrange("p a b -> p (a b)")),
          w=["c1sb"])
    c2sb = ctx.sbuf("c2sb", [16, 14336], BF16)
    p.dma("spd", lambda e: e.dma_start(out=c2sb[:], in_=c2T), w=["c2sb"])

    xbt = ctx.sbuf("xbt", [128, 2 * NT], BF16)
    u_ps = pool[0:16, 0:NT]
    for ic in range(32):
        bi = ic % 2
        p.dma("spd", lambda e, ic=ic, bi=bi: e.dma_start(
            out=xbt[:, bi * NT:(bi + 1) * NT],
            in_=xin[ic * 128:(ic + 1) * 128, :]), w=[f"xbt{bi}"])
        for h in range(2):
            p.op("pe", lambda e, ic=ic, bi=bi, h=h: e.matmul(
                u_ps[:, h * 512:(h + 1) * 512], c1sb[:, ic * 16:(ic + 1) * 16],
                xbt[:, bi * NT + h * 512:bi * NT + (h + 1) * 512],
                start=(ic == 0), stop=(ic == 31)),
                r=["c1sb", f"xbt{bi}"], w=["u_ps"])
    u_bf = ctx.sbuf("u_bf", [16, NT], BF16)
    p.op("act", lambda e: e.activation(u_bf[:], u_ps[:], AF.Copy),
         r=["u_ps"], w=["u_bf"])

    iota256 = ctx.sbuf("iota256", [128, 256], F32)
    p.op("pool", lambda e: e.iota(iota256[:], pattern=[[1, 256]], base=0,
                                  channel_multiplier=0,
                                  allow_small_or_imprecise_dtypes=True),
         w=["iota256"])
    kbase = ctx.sbuf("kbase", [128, 56], F32)
    p.op("pool", lambda e: e.iota(kbase[:], pattern=[[256, 56]], base=0,
                                  channel_multiplier=0,
                                  allow_small_or_imprecise_dtypes=True),
         w=["kbase"])

    xt_t = ctx.sbuf("xt_t", [128, DM], F32)
    g1 = ctx.sbuf("g1", [128, DM], F32)
    g2 = ctx.sbuf("g2", [128, DM], F32)
    acc = ctx.sbuf("acc", [128, DM], F32)
    junk = ctx.sbuf("junk", [128, DM], F32)
    outsb = ctx.sbuf("outsb", [128, DM], F32)
    m56 = ctx.sbuf("m56", [128, 56], F32)
    s56 = ctx.sbuf("s56", [128, 56], F32)
    h56 = ctx.sbuf("h56", [128, 56], F32)
    idx56 = ctx.sbuf("idx56", [128, 56], F32)
    idxu = ctx.sbuf("idxu", [128, 56], U32)
    eqb = ctx.sbuf("eqb", [128, 256], F32)
    int_ps = pool[:, 1024:2048]
    colA = ctx.sbuf("colA", [128, 4], F32)

    xfm_t = ctx.sbuf("xfm_t", [128, 128], F32) if xtm is None else None
    for tt in range(8):
        trows = slice(tt * 128, (tt + 1) * 128)
        if xtm is not None:
            p.dma("sp", lambda e, trows=trows: e.dma_start(
                out=xt_t[:], in_=xtm[trows, :]), r=["xt_rd"], w=["xt_t"])
        else:
            for mc in range(32):
                p.dma("sp", lambda e, mc=mc, tt=tt: e.dma_start(
                    out=xfm_t[:], in_=xin[mc * 128:(mc + 1) * 128,
                                          tt * 128:(tt + 1) * 128]),
                    r=["xfm_t"], w=["xfm_t"])
                p.op("pe", lambda e, mc=mc: e.transpose(
                    pool[:, 2048:2176], xfm_t[:], ident[:]),
                    r=["xfm_t", "ident", "DTR"], w=["DTR"])
                p.op("act", lambda e, mc=mc: e.activation(
                    xt_t[:, mc * 128:(mc + 1) * 128], pool[:, 2048:2176],
                    AF.Copy), r=["DTR", "xt_rd", "xt_t"], w=["xt_t"])
        if mode == "norouter":
            p.op("pool", lambda e: e.iota(idxu[:], pattern=[[256, 56]], base=0, channel_multiplier=0),
                 r=["idxu"], w=["idxu"])
        for kb in range(0 if mode != "norouter" else -1, 56 if mode != "norouter" else -1):
            bi = kb % 2
            psl = int_ps[:, bi * 512:bi * 512 + 256]
            p.op("pe", lambda e, kb=kb, tt=tt, psl=psl: e.matmul(
                psl, u_bf[:, tt * 128:(tt + 1) * 128],
                c2sb[:, kb * 256:(kb + 1) * 256], start=True, stop=True),
                r=["u_bf", "c2sb"], w=[f"IP{bi}"])
            p.op("dve", lambda e, kb=kb, psl=psl: e.reduce_max(
                m56[:, kb:kb + 1], psl, AX.X), r=[f"IP{bi}"], w=["m56"])
            p.op("dve", lambda e, kb=kb, psl=psl: e.tensor_scalar(
                eqb[:], psl, m56[:, kb:kb + 1], None, ALU.is_equal),
                r=[f"IP{bi}", "m56"], w=["eqb"])
            p.op("dve", lambda e, kb=kb: e.scalar_tensor_tensor(
                junk[:, 0:256], iota256[:], 1.0, eqb[:], ALU.mult, ALU.mult,
                accum_out=idx56[:, kb:kb + 1]),
                r=["iota256", "eqb"], w=["junk", "idx56"])
        if mode != "norouter":
            p.op("dve", lambda e: e.tensor_add(idx56[:], idx56[:], kbase[:]),
                 r=["idx56", "kbase"], w=["idx56"])
            p.op("dve", lambda e: e.tensor_copy(idxu[:], idx56[:]),
                 r=["idx56"], w=["idxu"])
        for kb in range(56):
            if mode == "nogather":
                p.op("dve", lambda e: e.memset(g1[:], 0.5), r=["g1"], w=["g1"])
            else:
                p.dma("spd", lambda e, kb=kb: e.indirect_dma_start(
                    out=g1[:], out_offset=None, in_=w1p,
                    in_offset=IndirectOffsetOnAxis(idxu[:, kb:kb + 1], 0)),
                    r=["idxu", "g1"], w=["g1"])
            p.op("dve", lambda e, kb=kb: e.scalar_tensor_tensor(
                junk[:], g1[:], 1.0, xt_t[:], ALU.mult, ALU.mult,
                accum_out=s56[:, kb:kb + 1]),
                r=["g1", "xt_t"], w=["junk", "s56"])
        p.op("dve", lambda e: e.tensor_scalar(h56[:], s56[:], 0.0, None, ALU.max),
             r=["s56"], w=["h56"])
        p.op("dve", lambda e: e.memset(acc[:], 0.0), w=["acc"])
        for kb in range(56):
            if mode == "nogather":
                p.op("dve", lambda e: e.memset(g2[:], 0.5), r=["g2"], w=["g2"])
            else:
                p.dma("spd", lambda e, kb=kb: e.indirect_dma_start(
                    out=g2[:], out_offset=None, in_=w2p,
                    in_offset=IndirectOffsetOnAxis(idxu[:, kb:kb + 1], 0)),
                    r=["idxu", "g2"], w=["g2"])
            p.op("dve", lambda e, kb=kb: e.scalar_tensor_tensor(
                acc[:], g2[:], h56[:, kb:kb + 1], acc[:], ALU.mult, ALU.add),
                r=["g2", "h56", "acc"], w=["acc"])
        p.op("dve", lambda e: e.tensor_add(acc[:], acc[:], xt_t[:]),
             r=["acc", "xt_t"], w=["acc", "xt_rd"])
        p.op("dve", lambda e: e.tensor_reduce(colA[:, 0:1], acc[:], AX.X, ALU.add),
             r=["acc"], w=["colA"])
        p.op("act", lambda e: e.activation(junk[:], acc[:], AF.Square),
             r=["acc"], w=["junk"])
        p.op("dve", lambda e: e.tensor_reduce(colA[:, 1:2], junk[:], AX.X, ALU.add),
             r=["junk"], w=["colA"])
        p.op("act", lambda e: e.activation(colA[:, 0:1], colA[:, 0:1], AF.Copy,
                                           scale=1.0 / DM), r=["colA"], w=["colA"])
        p.op("act", lambda e: e.activation(colA[:, 1:2], colA[:, 1:2], AF.Copy,
                                           scale=1.0 / DM), r=["colA"], w=["colA"])
        p.op("dve", lambda e: e.tensor_mul(colA[:, 2:3], colA[:, 0:1], colA[:, 0:1]),
             r=["colA"], w=["colA"])
        p.op("dve", lambda e: e.tensor_sub(colA[:, 1:2], colA[:, 1:2], colA[:, 2:3]),
             r=["colA"], w=["colA"])
        p.op("act", lambda e: e.activation(colA[:, 1:2], colA[:, 1:2], AF.Ln,
                                           bias=EPS), r=["colA", "consts"], w=["colA"])
        p.op("act", lambda e: e.activation(colA[:, 1:2], colA[:, 1:2], AF.Exp,
                                           scale=-0.5), r=["colA"], w=["colA"])
        p.op("dve", lambda e: e.tensor_scalar(
            outsb[:], acc[:], colA[:, 0:1], colA[:, 1:2], ALU.subtract, ALU.mult),
            r=["acc", "colA"], w=["outsb"])
        p.dma("sp", lambda e, trows=trows: e.dma_start(
            out=out_tm[trows, :], in_=outsb[:]), r=["outsb"], w=["xout"])


def build_D(mode="full"):
    ctx = Ctx()
    ctx.load_consts()
    pool = ctx.psum("pool", [128, 3584], F32)
    xin = ctx.din("xin", [DM, NT])
    xtm = ctx.din("xtm", [NT, DM])
    out_tm = ctx.dout("out_tm", [NT, DM])
    emit_D(ctx, pool, xin, out_tm, xtm=xtm, mode=mode)
    return ctx.build(["xout"])


def host_prep_D(inp):
    c1p = _pack_inchunks(inp["L1_ffn_c1_w"].T[PI])
    c2T = np.ascontiguousarray(inp["L1_ffn_c2_w"].T)
    w1p = np.ascontiguousarray(inp["L1_ffn_w1"][:, PI])
    w2p = np.ascontiguousarray(inp["L1_ffn_w2t"][:, PI])
    return {"c1p": c1p, "c2T": c2T, "w1p": w1p, "w2p": w2p,
            "consts_in": CONSTS_HOST}



def emit_C(ctx, pool, poolb, xin, xout, ident):
    nc, p = ctx.nc, ctx.p
    wqp = ctx.din("wqp", [128, 32 * 4096])
    wkp = ctx.din("wkp", [128, 32 * 1024])
    wvp = ctx.din("wvp", [128, 32 * 1024])
    bqp = ctx.din("bqp", [128, 32])
    bkp = ctx.din("bkp", [128, 8])
    bvp = ctx.din("bvp", [128, 8])
    xout = xout
    wqs_d = ctx.dram("wqs_d", [32, 128, 1024], BF16)

    TT4 = 256
    NSUB = NT // TT4

    ident_bf = ctx.sbuf("ident_bf", [128, 128], BF16)
    p.op("dve", lambda e: e.tensor_copy(ident_bf[:], ident[:]),
         r=["ident"], w=["identbf"])

    wqt = ctx.sbuf("wqt", [128, 4096], BF16)
    wqs_t = ctx.sbuf("wqs_t", [128, 1024], BF16)
    for ic in range(32):
        p.dma("spd", lambda e, ic=ic: e.dma_start(
            out=wqt[:], in_=wqp[:, ic * 4096:(ic + 1) * 4096]),
            r=["wqs_t"], w=["wqt"])
        v3 = wqt[:].rearrange("p (h j d) -> p h j d", h=8, j=4)
        p.op("dve", lambda e, v3=v3: e.tensor_add(
            wqs_t[:].rearrange("p (h d) -> p h d", h=8),
            v3[:, :, 0, :], v3[:, :, 1, :]), r=["wqt"], w=["wqs_t"])
        p.op("dve", lambda e, v3=v3: e.tensor_add(
            wqs_t[:].rearrange("p (h d) -> p h d", h=8),
            wqs_t[:].rearrange("p (h d) -> p h d", h=8), v3[:, :, 2, :]),
            r=["wqt", "wqs_t"], w=["wqs_t"])
        p.op("dve", lambda e, v3=v3: e.tensor_add(
            wqs_t[:].rearrange("p (h d) -> p h d", h=8),
            wqs_t[:].rearrange("p (h d) -> p h d", h=8), v3[:, :, 3, :]),
            r=["wqt", "wqs_t"], w=["wqs_t"])
        p.dma("spd", lambda e, ic=ic: e.dma_start(
            out=wqs_d[ic], in_=wqs_t[:]), r=["wqs_t"], w=["wqs_d"])

    bq_sb = ctx.sbuf("bq_sb", [128, 32], F32)
    p.dma("sp", lambda e: e.dma_start(out=bq_sb[:], in_=bqp), w=["bq_sb"])
    bqs = ctx.sbuf("bqs", [128, 8], F32)
    bq3 = bq_sb[:].rearrange("p (h j) -> p h j", h=8)
    p.op("dve", lambda e: e.tensor_add(bqs[:], bq3[:, :, 0], bq3[:, :, 1]),
         r=["bq_sb"], w=["bqs"])
    p.op("dve", lambda e: e.tensor_add(bqs[:], bqs[:], bq3[:, :, 2]),
         r=["bq_sb", "bqs"], w=["bqs"])
    p.op("dve", lambda e: e.tensor_add(bqs[:], bqs[:], bq3[:, :, 3]),
         r=["bq_sb", "bqs"], w=["bqs"])
    bk_sb = ctx.sbuf("bk_sb", [128, 8], F32)
    p.dma("sp", lambda e: e.dma_start(out=bk_sb[:], in_=bkp), w=["bk_sb"])
    bv_sb = ctx.sbuf("bv_sb", [128, 8], F32)
    p.dma("sp", lambda e: e.dma_start(out=bv_sb[:], in_=bvp), w=["bv_sb"])

    q_pad = ctx.sbuf("q_pad", [128, 32 * TT4], BF16)
    k_pad = ctx.sbuf("k_pad", [128, 32 * TT4], BF16)
    v_pad = ctx.sbuf("v_pad", [128, 9 * TT4], BF16)
    p.op("dve", lambda e: e.memset(
        q_pad[:].rearrange("p (t s) -> p t s", s=32)[:, :, 9:32], 0.0), w=["qz"])
    p.op("dve", lambda e: e.memset(
        k_pad[:].rearrange("p (t s) -> p t s", s=32)[:, :, 9:32], 0.0), w=["kz"])
    p.op("dve", lambda e: e.memset(
        k_pad[:].rearrange("p (t s) -> p t s", s=32)[:, :, 8], 1.0), w=["kpad"])
    p.op("dve", lambda e: e.memset(
        v_pad[:].rearrange("p (t s) -> p t s", s=9)[:, :, 8], 1.0), w=["vz"])

    wtile = ctx.sbuf("wtile", [128, 2 * 1024], BF16)
    xtile = ctx.sbuf("xtile", [128, 2 * TT4], BF16)
    prod = ctx.sbuf("prod", [128, NT * 8], F32)
    sqb = ctx.sbuf("sqb", [128, 8 * TT4], BF16)
    qn = ctx.sbuf("qn", [128, TT4], F32)
    kn2b = ctx.sbuf("kn2b", [128, TT4], BF16)
    kfrow = ctx.sbuf("kfrow", [1, TT4], F32)
    kfs = ctx.sbuf("kfs", [1, 2], F32)
    ones_bf1 = ctx.sbuf("ones_bf1", [128, 1], BF16)
    p.op("dve", lambda e: e.memset(ones_bf1[:], 1.0), w=["ones1"])
    ones_row = ctx.sbuf("ones_row", [1, 128], BF16)
    p.op("dve", lambda e: e.memset(ones_row[:], 1.0), w=["ones1"])
    kfs_bf = ctx.sbuf("kfs_bf", [1, 1], BF16)
    nkf_col = ctx.sbuf("nkf_col", [128, 1], F32)
    qT = ctx.sbuf("qT", [128, 2 * 128], BF16)
    kT = ctx.sbuf("kT", [128, 2 * 128], BF16)
    expT = ctx.sbuf("expT", [128, 2 * 512], BF16)
    og_sb = ctx.sbuf("og_sb", [128, 8 * 9], F32)
    rcp = ctx.sbuf("rcp", [128, TT4], F32)

    ps = pool[:, 0:3072]
    psb = poolb

    def PSL(bank, w=512):
        return ps[:, bank * 512:bank * 512 + w]

    for st in range(NSUB):
        tsl = slice(st * TT4, (st + 1) * TT4)
        for proj, wsrc, nhk, pad, bias in (
                ("q", wqs_d, 8, q_pad, bqs),
                ("k", wkp, 8, k_pad, bk_sb),
                ("v", wvp, 8, v_pad, bv_sb)):
            for hkh in range(2):
                for ic in range(32):
                    bi = ic % 2
                    if proj == "q":
                        p.dma("spd", lambda e, ic=ic, bi=bi: e.dma_start(
                            out=wtile[:, bi * 1024:(bi + 1) * 1024], in_=wqs_d[ic]),
                            r=["wqs_d", f"wt{bi}"], w=[f"wt{bi}"])
                    else:
                        p.dma("spd", lambda e, ic=ic, bi=bi, wsrc=wsrc: e.dma_start(
                            out=wtile[:, bi * 1024:(bi + 1) * 1024],
                            in_=wsrc[:, ic * 1024:(ic + 1) * 1024]),
                            r=[f"wt{bi}"], w=[f"wt{bi}"])
                    p.dma("spd", lambda e, ic=ic, bi=bi, tsl=tsl: e.dma_start(
                        out=xtile[:, bi * TT4:(bi + 1) * TT4],
                        in_=xin[ic * 128:(ic + 1) * 128, tsl]),
                        r=[f"xt{bi}"], w=[f"xt{bi}"])
                    for hh in range(4):
                        hk = hkh * 4 + hh
                        p.op("pe", lambda e, ic=ic, bi=bi, hk=hk, hh=hh: e.matmul(
                            ps[:, hh * 512:hh * 512 + TT4],
                            wtile[:, bi * 1024 + hk * 128:bi * 1024 + (hk + 1) * 128],
                            xtile[:, bi * TT4:(bi + 1) * TT4],
                            start=(ic == 0), stop=(ic == 31)),
                            r=[f"wt{bi}", f"xt{bi}"], w=[f"PP{hh}"])
                ns = 9 if proj == "v" else 32
                for hh in range(4):
                    hk = hkh * 4 + hh
                    p.op("dve", lambda e, hk=hk, hh=hh, pad=pad, bias=bias,
                         ns=ns: e.tensor_scalar(
                             pad[:].rearrange("p (t s) -> p t s", s=ns)[:, :, hk],
                             ps[:, hh * 512:hh * 512 + TT4],
                             bias[:, hk:hk + 1], None, ALU.add),
                         r=[f"PP{hh}", "bqs", "bk_sb", "bv_sb"],
                         w=[proj + "pad"])
        p.op("act", lambda e: e.activation(
            sqb[:].rearrange("p (t h) -> p t h", h=8),
            q_pad[:].rearrange("p (t s) -> p t s", s=32)[:, :, 0:8], AF.Square),
            r=["qpad"], w=["sqb"])
        p.op("dve", lambda e: e.tensor_reduce(
            qn[:], sqb[:].rearrange("p (t h) -> p t h", h=8), AX.X, ALU.add),
            r=["sqb"], w=["qn"])
        p.op("act", lambda e: e.activation(qn[:], qn[:], AF.Sqrt),
             r=["qn"], w=["qn"])
        p.op("act", lambda e: e.activation(
            sqb[:].rearrange("p (t h) -> p t h", h=8),
            k_pad[:].rearrange("p (t s) -> p t s", s=32)[:, :, 0:8], AF.Square),
            r=["kpad"], w=["sqb"])
        def _kn2(e):
            with ctx.nc.allow_low_precision(reason="cauchy safety bound only"):
                return e.tensor_reduce(
                    kn2b[:], sqb[:].rearrange("p (t h) -> p t h", h=8),
                    AX.X, ALU.add)
        p.op("dve", _kn2, r=["sqb"], w=["kn2b"])
        p.op("pe", lambda e: e.matmul(ps[0:1, 2816:2816 + TT4], ones_bf1[:],
                                      kn2b[:], start=True, stop=True),
             r=["kn2b", "ones1", "OG"], w=["OG"])
        p.op("act", lambda e: e.activation(kfrow[:], ps[0:1, 2816:2816 + TT4],
                                           AF.Sqrt), r=["OG"], w=["kfrow"])
        p.op("dve", lambda e: e.reduce_max(kfs[:, 0:1], kfrow[:], AX.X),
             r=["kfrow"], w=["kfs"])
        p.op("act", lambda e: e.activation(kfs[:, 1:2], kfs[:, 0:1], AF.Copy,
                                           scale=-1.0), r=["kfs"], w=["kfs"])
        p.op("dve", lambda e: e.tensor_copy(kfs_bf[:], kfs[:, 1:2]),
             r=["kfs"], w=["kfsbf"])
        p.op("pe", lambda e: e.matmul(ps[:, 2688:2689], ones_row[:],
                                      kfs_bf[:], start=True, stop=True),
             r=["kfsbf", "ones1", "OG"], w=["OG"])
        p.op("act", lambda e: e.activation(nkf_col[:], ps[:, 2688:2689],
                                           AF.Copy), r=["OG"], w=["nkf"])
        p.op("dve", lambda e: e.tensor_scalar(
            q_pad[:].rearrange("p (t s) -> p t s", s=32)[:, :, 8],
            qn[:], nkf_col[:], None, ALU.mult),
            r=["qn", "nkf"], w=["qpad"])
        for g in range(TT4 // 4):
            gb = g % 2
            qv = q_pad[:, g * 128:(g + 1) * 128]
            kv = k_pad[:, g * 128:(g + 1) * 128]
            p.op("pe", lambda e, qv=qv, gb=gb: e.transpose(
                psb[:, gb * 512:gb * 512 + 128], qv, ident_bf[:]),
                r=["qpad", "qz", "identbf"], w=[f"PSB{gb}"])
            p.op("act", lambda e, gb=gb: e.activation(
                qT[:, gb * 128:(gb + 1) * 128], psb[:, gb * 512:gb * 512 + 128],
                AF.Copy), r=[f"PSB{gb}"], w=[f"qT{gb}"])
            p.op("pe", lambda e, kv=kv, gb=gb: e.transpose(
                psb[:, gb * 512 + 128:gb * 512 + 256], kv, ident_bf[:]),
                r=["kpad", "kz", "identbf"], w=[f"PSB{gb}"])
            p.op("act", lambda e, gb=gb: e.activation(
                kT[:, gb * 128:(gb + 1) * 128],
                psb[:, gb * 512 + 128:gb * 512 + 256],
                AF.Copy), r=[f"PSB{gb}"], w=[f"kT{gb}"])
            for j in range(4):
                p.op("pe", lambda e, j=j, gb=gb: e.matmul(
                    ps[:, 2048 + j * 128:2048 + (j + 1) * 128],
                    kT[32 * j:32 * j + 32, gb * 128:(gb + 1) * 128],
                    qT[32 * j:32 * j + 32, gb * 128:(gb + 1) * 128],
                    start=True, stop=True, tile_position=(32 * j, 0)),
                    r=[f"kT{gb}", f"qT{gb}", "SC"], w=["SC"])
            p.op("act", lambda e, gb=gb: e.activation(
                expT[:, gb * 512:(gb + 1) * 512],
                ps[:, 2048:2560],
                AF.Exp, scale=INV_SQRT128), r=["SC"], w=[f"expT{gb}"])
            for j in range(4):
                t = 4 * g + j
                vv = v_pad[:, t * 9:(t + 1) * 9]
                p.op("pe", lambda e, j=j, g=g, gb=gb, vv=vv: e.matmul(
                    ps[:, 2560 + ((4 * g + j) % 8) * 16:
                       2560 + ((4 * g + j) % 8) * 16 + 9],
                    expT[:, gb * 512 + j * 128:gb * 512 + (j + 1) * 128],
                    vv, start=True, stop=True),
                    r=[f"expT{gb}", "vpad", "vz"], w=["OG"])
            if g % 2 == 1:
                t0 = 4 * (g - 1)
                p.op("act", lambda e, t0=t0: e.activation(
                    og_sb[:].rearrange("p (t n) -> p t n", t=8),
                    ps[:, 2560:2688].rearrange("p (t n) -> p t n", t=8)[:, :, 0:9],
                    AF.Copy), r=["OG"], w=["og_sb"])
                p.op("dve", lambda e, t0=t0, st=st: e.tensor_copy(
                    prod[:, (st * TT4 + t0) * 8:(st * TT4 + t0 + 8) * 8]
                    .rearrange("p (t n) -> p t n", t=8),
                    og_sb[:].rearrange("p (t n) -> p t n", t=8)[:, :, 0:8]),
                    r=["og_sb"], w=["prodraw"])
                p.op("dve", lambda e, t0=t0, st=st: e.tensor_copy(
                    rcp[:, t0:t0 + 8],
                    og_sb[:].rearrange("p (t n) -> p t n", t=8)[:, :, 8]),
                    r=["og_sb"], w=["rcpraw"])
        p.op("dve", lambda e: e.reciprocal(rcp[:], rcp[:]),
             r=["rcpraw"], w=["rcps"])
        p.op("dve", lambda e, st=st: e.tensor_tensor(
            prod[:, st * TT4 * 8:(st + 1) * TT4 * 8]
            .rearrange("p (t n) -> p t n", t=TT4),
            prod[:, st * TT4 * 8:(st + 1) * TT4 * 8]
            .rearrange("p (t n) -> p t n", t=TT4),
            rcp[:].unsqueeze(2).broadcast_to([128, TT4, 8]), ALU.mult),
            r=["prodraw", "rcps"], w=["prod"])

    xin_t = ctx.sbuf("xin_t", [128, 2 * 512], F32)

    def emit_pre(oc, th, dst, dres):
        hk = oc // 4
        xsl = xin_t[:, th * 512:(th + 1) * 512]
        p.dma("sp", lambda e, oc=oc, th=th, xsl=xsl: e.dma_start(
            out=xsl, in_=xin[oc * 128:(oc + 1) * 128, th * 512:(th + 1) * 512]),
            w=[f"xin_t{th}"])
        pv = prod[:].rearrange("p (t n) -> p t n", t=NT)[:, th * 512:(th + 1) * 512, hk]
        p.op("dve", lambda e, pv=pv, xsl=xsl, dst=dst: e.tensor_add(
            dst, xsl, pv), r=["prod", f"xin_t{th}"], w=[dres])

    fm_stage_ln(ctx, "cln", 512, 2, xout, emit_pre,
                mean_ps=ps[:, 0:1024], e2_ps=ps[:, 1024:2048],
                mres="PP0", eres="PP1")


def build_C():
    ctx = Ctx()
    ctx.load_consts()
    identd = ctx.din("identd", [128, 128], F32)
    ident = ctx.sbuf("identC", [128, 128], F32)
    ctx.p.dma("spd", lambda e: e.dma_start(out=ident[:], in_=identd), w=["ident"])
    pool = ctx.psum("pool", [128, 3584], F32)
    poolb = ctx.psum("poolb", [128, 1024], BF16)
    xin = ctx.din("xin", [DM, NT])
    xout = ctx.dout("xout", [DM, NT])
    emit_C(ctx, pool, poolb, xin, xout, ident)
    return ctx.build(["xout"])


def host_prep_C(inp):
    wq = np.asarray(inp["L1_attn_Wq"])[PI]
    wqp = wq.reshape(32, 128, 128, 32).transpose(1, 0, 3, 2)
    wqp = np.ascontiguousarray(wqp.reshape(128, 32 * 4096))
    wk = np.asarray(inp["L1_attn_Wk"])[PI]
    wkp = np.ascontiguousarray(
        wk.reshape(32, 128, 128, 8).transpose(1, 0, 3, 2).reshape(128, 32 * 1024))
    wv = np.asarray(inp["L1_attn_Wv"])[PI]
    wvp = np.ascontiguousarray(
        wv.reshape(32, 128, 128, 8).transpose(1, 0, 3, 2).reshape(128, 32 * 1024))
    return {"wqp": wqp, "wkp": wkp, "wvp": wvp,
            "bqp": np.ascontiguousarray(inp["L1_attn_bq"]),
            "bkp": np.ascontiguousarray(inp["L1_attn_bk"]),
            "bvp": np.ascontiguousarray(inp["L1_attn_bv"]),
            "identd": np.eye(128, dtype=np.float32),
            "consts_in": CONSTS_HOST}



def emit_A(ctx, pool, xbat, xown, ownq, xout):
    nc, p = ctx.nc, ctx.p
    LB = 4096
    winp = ctx.din("winp", [128, 32, 512])
    w1tp = ctx.din("w1tp", [4, 128, 4 * 2048])
    w2tp = ctx.din("w2tp", [4, 128, 16 * 512])
    woutp = ctx.din("woutp", [128, 4 * DM])
    ln1w = ctx.din("ln1w", [128, 16]);  ln1b = ctx.din("ln1b", [128, 16])
    ln2w = ctx.din("ln2w", [128, 16]);  ln2b = ctx.din("ln2b", [128, 16])
    sclc = ctx.din("sclc", [128, 16])
    phre = ctx.din("phre", [128, 16]);  phim = ctx.din("phim", [128, 16])
    lcre = ctx.din("lcre", [128, 16]);  lcim = ctx.din("lcim", [128, 16])
    binc = ctx.din("binc", [128, 4])
    b1c = ctx.din("b1c", [128, 64])
    b2r = ctx.din("b2r", [1, 2048])
    lnlw = ctx.din("lnlw", [128, 4]);  lnlb = ctx.din("lnlb", [128, 4])
    boutc = ctx.din("boutc", [128, 32])

    def ld(name, ap, shape, dt=F32):
        t = ctx.sbuf(name + "_sb", list(shape), dt)
        p.dma("sp", lambda e, t=t, ap=ap: e.dma_start(out=t[:], in_=ap),
              w=["cols"])
        return t
    ln1w_s = ld("ln1w", ln1w, [128, 16]); ln1b_s = ld("ln1b", ln1b, [128, 16])
    ln2w_s = ld("ln2w", ln2w, [128, 16]); ln2b_s = ld("ln2b", ln2b, [128, 16])
    scl_s = ld("sclc", sclc, [128, 16])
    phre_s = ld("phre", phre, [128, 16]); phim_s = ld("phim", phim, [128, 16])
    lcre_s = ld("lcre", lcre, [128, 16]); lcim_s = ld("lcim", lcim, [128, 16])
    binc_s = ld("binc", binc, [128, 4])
    b1c_s = ld("b1c", b1c, [128, 64])
    b2r_s = ld("b2r", b2r, [1, 2048])
    lnlw_s = ld("lnlw", lnlw, [128, 4]); lnlb_s = ld("lnlb", lnlb, [128, 4])
    boutc_s = ld("boutc", boutc, [128, 32])
    ownq_s = ld("ownq", ownq, [128, 4])

    rcol = ctx.sbuf("rcol", [128, 16], F32)
    thcol = ctx.sbuf("thcol", [128, 16], F32)
    tmpc = ctx.sbuf("tmpc", [128, 16], F32)
    tmpd = ctx.sbuf("tmpd", [128, 16], F32)
    p.op("act", lambda e: e.activation(rcol[:], phre_s[:], AF.Square),
         r=["cols"], w=["rcol"])
    p.op("act", lambda e: e.activation(tmpc[:], phim_s[:], AF.Square),
         r=["cols"], w=["tmpc"])
    p.op("dve", lambda e: e.tensor_add(rcol[:], rcol[:], tmpc[:]),
         r=["rcol", "tmpc"], w=["rcol"])
    p.op("act", lambda e: e.activation(rcol[:], rcol[:], AF.Sqrt),
         r=["rcol"], w=["rcol"])
    p.op("act", lambda e: e.activation(rcol[:], rcol[:], AF.Tanh),
         r=["rcol"], w=["rcol"])
    tmpe = ctx.sbuf("tmpe", [128, 16], F32)
    tmpf = ctx.sbuf("tmpf", [128, 16], F32)
    p.op("dve", lambda e: e.reciprocal(tmpf[:], phre_s[:]),
         r=["cols"], w=["tmpf"])
    p.op("dve", lambda e: e.tensor_mul(thcol[:], phim_s[:], tmpf[:]),
         r=["cols", "tmpf"], w=["thcol"])
    p.op("act", lambda e: e.activation(tmpe[:], thcol[:], AF.Sign),
         r=["thcol"], w=["tmpe"])
    p.op("act", lambda e: e.activation(thcol[:], thcol[:], AF.Abs),
         r=["thcol"], w=["thcol"])
    p.op("dve", lambda e: e.tensor_scalar(tmpf[:], thcol[:], 1e-30, None,
                                          ALU.max), r=["thcol"], w=["tmpf"])
    p.op("dve", lambda e: e.reciprocal(tmpf[:], tmpf[:]),
         r=["tmpf"], w=["tmpf"])
    p.op("dve", lambda e: e.tensor_scalar(tmpd[:], thcol[:], 1.0, None,
                                          ALU.is_gt), r=["thcol"], w=["tmpd"])
    p.op("dve", lambda e: e.tensor_tensor(thcol[:], thcol[:], tmpf[:],
                                          ALU.min), r=["thcol", "tmpf"],
         w=["thcol"])
    p.op("act", lambda e: e.activation(thcol[:], thcol[:], AF.Arctan),
         r=["thcol"], w=["thcol"])
    p.op("dve", lambda e: e.scalar_tensor_tensor(
        tmpf[:], thcol[:], -2.0, tmpd[:], ALU.mult, ALU.bypass),
        r=["thcol", "tmpd"], w=["tmpf"]) if False else None
    p.op("dve", lambda e: e.tensor_scalar(tmpf[:], thcol[:], -2.0, PIHALF,
                                          ALU.mult, ALU.add),
         r=["thcol", "consts"], w=["tmpf"])
    p.op("dve", lambda e: e.tensor_mul(tmpf[:], tmpf[:], tmpd[:]),
         r=["tmpf", "tmpd"], w=["tmpf"])
    p.op("dve", lambda e: e.tensor_add(thcol[:], thcol[:], tmpf[:]),
         r=["thcol", "tmpf"], w=["thcol"])
    p.op("dve", lambda e: e.tensor_mul(thcol[:], thcol[:], tmpe[:]),
         r=["thcol", "tmpe"], w=["thcol"])
    p.op("act", lambda e: e.activation(tmpc[:], phim_s[:], AF.Sign),
         r=["cols"], w=["tmpc"])
    p.op("dve", lambda e: e.tensor_scalar(tmpd[:], phre_s[:], 0.0, None,
                                          ALU.is_lt), r=["cols"], w=["tmpd"])
    p.op("dve", lambda e: e.tensor_mul(tmpc[:], tmpc[:], tmpd[:]),
         r=["tmpc", "tmpd"], w=["tmpc"])
    p.op("dve", lambda e: e.scalar_tensor_tensor(
        thcol[:], tmpc[:], float(np.pi), thcol[:], ALU.mult, ALU.add),
        r=["tmpc", "thcol", "consts"], w=["thcol"])

    ps = pool
    iot = ctx.sbuf("iot", [128, LB], F32)
    p.op("pool", lambda e: e.iota(iot[:], pattern=[[1, LB]], base=1,
                                  channel_multiplier=0,
                                  allow_small_or_imprecise_dtypes=True),
         w=["iot"])
    ones512 = ctx.sbuf("ones512", [1, 512], F32)
    p.op("dve", lambda e: e.memset(ones512[:], 1.0), w=["ones512"])
    ones_bf = ctx.sbuf("onesA", [128, 128], BF16)
    p.op("dve", lambda e: e.memset(ones_bf[:], 1.0), w=["onesA"])

    h = ctx.sbuf("h", [128, 4 * LB], F32)
    ybf = ctx.sbuf("ybf", [128, 4 * LB], BF16)
    WB = ctx.sbuf("WBb", [128, 16384], BF16)
    fbuf = ctx.sbuf("fbuf", [128, 8 * 512], BF16)
    TA = ctx.sbuf("TA", [128, 1024], F32)
    TB = ctx.sbuf("TB", [128, 1024], F32)
    WR = ctx.sbuf("WR", [128, 1024], F32)
    UR = ctx.sbuf("UR", [128, 1024], F32)
    CBt = ctx.sbuf("CBt", [128, 1024], BF16)
    SBt = ctx.sbuf("SBt", [128, 1024], BF16)
    ucar = ctx.sbuf("ucar", [128, 2], F32)
    hbf4 = ctx.sbuf("hbf4", [128, 2048], BF16)
    sqb4 = ctx.sbuf("sqb4", [128, 2048], BF16)
    mu4 = ctx.sbuf("mu4", [128, 512], F32)
    rs4 = ctx.sbuf("rs4", [128, 512], F32)
    wtl = ctx.sbuf("wtl", [128, 2 * 512], BF16)
    xtl = ctx.sbuf("xtl", [128, 2 * 512], BF16)

    for tq in range(8):
        tsl = slice(tq * 512, (tq + 1) * 512)
        for ic in range(32):
            bi = ic % 2
            p.dma("spd", lambda e, ic=ic, bi=bi: e.dma_start(
                out=wtl[:, bi * 512:(bi + 1) * 512],
                in_=winp[:, ic, :]), r=[f"wtl{bi}"], w=[f"wtl{bi}"])
            p.dma("spd", lambda e, ic=ic, bi=bi, tsl=tsl: e.dma_start(
                out=xtl[:, bi * 512:(bi + 1) * 512],
                in_=xbat[ic * 128:(ic + 1) * 128, tsl]),
                r=[f"xtl{bi}"], w=[f"xtl{bi}"])
            for dd in range(4):
                p.op("pe", lambda e, ic=ic, bi=bi, dd=dd: e.matmul(
                    ps[:, 1024 + dd * 512:1024 + (dd + 1) * 512],
                    wtl[:, bi * 512 + dd * 128:bi * 512 + (dd + 1) * 128],
                    xtl[:, bi * 512:(bi + 1) * 512],
                    start=(ic == 0), stop=(ic == 31)),
                    r=[f"wtl{bi}", f"xtl{bi}"], w=[f"HB{dd}"])
        for dd in range(4):
            p.op("dve", lambda e, dd=dd, tq=tq: e.tensor_scalar(
                h[:, dd * LB + tq * 512:dd * LB + (tq + 1) * 512],
                ps[:, 1024 + dd * 512:1024 + (dd + 1) * 512],
                binc_s[:, dd:dd + 1], None, ALU.add),
                r=[f"HB{dd}", "cols"], w=["h"])

    def ln512(wcol_s, bcol_s, lay, silu, src_res="h"):
        for tc in range(8):
            tsl = slice(tc * 512, (tc + 1) * 512)
            for ch in range(4):
                p.op("dve", lambda e, ch=ch, tsl=tsl: e.tensor_copy(
                    hbf4[:, ch * 512:(ch + 1) * 512], h[:, ch * LB:][:, tsl]),
                    r=["h"], w=["hbf4"])
            p.op("act", lambda e: e.activation(sqb4[:], hbf4[:], AF.Square),
                 r=["hbf4"], w=["sqb4"])
            for ch in range(4):
                p.op("pe", lambda e, ch=ch: e.matmul(
                    ps[:, 0:512], ones_bf[:], hbf4[:, ch * 512:(ch + 1) * 512],
                    start=(ch == 0), stop=(ch == 3)),
                    r=["hbf4", "onesA"], w=["B0"])
                p.op("pe", lambda e, ch=ch: e.matmul(
                    ps[:, 512:1024], ones_bf[:], sqb4[:, ch * 512:(ch + 1) * 512],
                    start=(ch == 0), stop=(ch == 3)),
                    r=["sqb4", "onesA"], w=["B1"])
            p.op("act", lambda e: e.activation(mu4[:], ps[:, 0:512], AF.Copy,
                                               scale=1.0 / 512),
                 r=["B0"], w=["mu4"])
            p.op("act", lambda e: e.activation(rs4[:], ps[:, 512:1024], AF.Copy,
                                               scale=1.0 / 512),
                 r=["B1"], w=["rs4"])
            p.op("dve", lambda e: e.tensor_mul(TA[:, 0:512], mu4[:], mu4[:]),
                 r=["mu4"], w=["TA"])
            p.op("dve", lambda e: e.tensor_sub(rs4[:], rs4[:], TA[:, 0:512]),
                 r=["rs4", "TA"], w=["rs4"])
            p.op("act", lambda e: e.activation(rs4[:], rs4[:], AF.Ln, bias=EPS),
                 r=["rs4", "consts"], w=["rs4"])
            p.op("act", lambda e: e.activation(rs4[:], rs4[:], AF.Exp, scale=-0.5),
                 r=["rs4"], w=["rs4"])
            for ch in range(4):
                dst = TB[:, 0:512]
                p.op("dve", lambda e, ch=ch, tsl=tsl: e.tensor_sub(
                    TB[:, 0:512], h[:, ch * LB:][:, tsl], mu4[:]),
                    r=["h", "mu4", "TB"], w=["TB"])
                p.op("dve", lambda e: e.tensor_mul(TB[:, 0:512], TB[:, 0:512],
                                                   rs4[:]),
                     r=["TB", "rs4"], w=["TB"])
                p.op("dve", lambda e, ch=ch, lay=lay, wcol_s=wcol_s,
                     bcol_s=bcol_s: e.tensor_scalar(
                         TB[:, 0:512], TB[:, 0:512],
                         wcol_s[:, lay * 4 + ch:lay * 4 + ch + 1],
                         bcol_s[:, lay * 4 + ch:lay * 4 + ch + 1],
                         ALU.mult, ALU.add),
                     r=["TB", "cols"], w=["TB"])
                if silu:
                    p.op("act", lambda e, ch=ch, tsl=tsl: e.activation(
                        ybf[:, ch * LB:][:, tsl], TB[:, 0:512], AF.Silu),
                        r=["TB"], w=["ybf"])
                else:
                    p.op("dve", lambda e, ch=ch, tsl=tsl: e.tensor_copy(
                        ybf[:, ch * LB:][:, tsl], TB[:, 0:512]),
                        r=["TB"], w=["ybf"])

    for lay in range(4):
        ln512(ln1w_s, ln1b_s, lay, True)
        for ch in range(4):
            s16 = lay * 4 + ch
            for qq in range(4):
                lsl = slice(qq * 1024, (qq + 1) * 1024)
                p.op("dve", lambda e, qq=qq, s16=s16, lsl=lsl: e.tensor_scalar(
                    TA[:], iot[:, lsl], thcol[:, s16:s16 + 1], None, ALU.mult),
                    r=["iot", "thcol", "TA"], w=["TA"])
                p.op("act", lambda e: e.activation(TB[:], TA[:], AF.Copy,
                                                   scale=INV_2PI),
                     r=["TA"], w=["TB"])
                p.op("act", lambda e: e.activation(TB[:], TB[:], AF.Copy,
                                                   bias=MAGIC), r=["TB", "consts"],
                     w=["TB"])
                p.op("act", lambda e: e.activation(TB[:], TB[:], AF.Copy,
                                                   bias=-MAGIC), r=["TB", "consts"],
                     w=["TB"])
                p.op("dve", lambda e: e.scalar_tensor_tensor(
                    TA[:], TB[:], -TWO_PI, TA[:], ALU.mult, ALU.add),
                    r=["TB", "TA", "consts"], w=["TA"])
                p.op("dve", lambda e: e.tensor_scalar(
                    TA[:], TA[:], float(np.pi), -float(np.pi),
                    ALU.min, ALU.max), r=["TA", "consts"], w=["TA"])
                p.op("act", lambda e: e.activation(TB[:], TA[:], AF.Abs),
                     r=["TA"], w=["TB"])
                p.op("act", lambda e: e.activation(CBt[:], TB[:], AF.Sin,
                                                   scale=-1.0, bias=PIHALF),
                     r=["TB", "consts"], w=["CBt"])
                p.op("act", lambda e: e.activation(SBt[:], TA[:], AF.Sin,
                                                   scale=-1.0),
                     r=["TA"], w=["SBt"])
                p.op("dve", lambda e, s16=s16: e.tensor_scalar(
                    TB[:], TB[:], 0.0, rcol[:, s16:s16 + 1], ALU.mult, ALU.add),
                    r=["TB", "rcol"], w=["TB"])
                p.op("dve", lambda e, ch=ch, lsl=lsl: e.tensor_mul(
                    WR[:], ybf[:, ch * LB:][:, lsl], CBt[:]),
                    r=["ybf", "CBt"], w=["WR"])
                if qq == 0:
                    ini_r = lcre_s[:, s16:s16 + 1]
                    ini_i = lcim_s[:, s16:s16 + 1]
                    rr = ["cols"]
                else:
                    ini_r = ucar[:, 0:1]
                    ini_i = ucar[:, 1:2]
                    rr = ["ucar"]
                p.op("dve", lambda e, ini_r=ini_r: e.tensor_tensor_scan(
                    TA[:], TB[:], WR[:], ini_r, ALU.mult, ALU.add),
                    r=["TB", "WR", "TA"] + rr, w=["TA"])
                p.op("dve", lambda e: e.tensor_copy(ucar[:, 0:1], TA[:, 1023:1024]),
                     r=["TA"], w=["ucar"])
                p.op("dve", lambda e, ch=ch, lsl=lsl: e.tensor_mul(
                    WR[:], ybf[:, ch * LB:][:, lsl], SBt[:]),
                    r=["ybf", "SBt", "WR"], w=["WR"])
                p.op("dve", lambda e, ini_i=ini_i: e.tensor_tensor_scan(
                    UR[:], TB[:], WR[:], ini_i, ALU.mult, ALU.add),
                    r=["TB", "WR"] + rr, w=["UR"])
                p.op("dve", lambda e: e.tensor_copy(ucar[:, 1:2], UR[:, 1023:1024]),
                     r=["UR"], w=["ucar"])
                p.op("dve", lambda e: e.tensor_mul(WR[:], TA[:], CBt[:]),
                     r=["TA", "CBt"], w=["WR"])
                p.op("dve", lambda e: e.tensor_mul(TB[:], UR[:], SBt[:]),
                     r=["UR", "SBt", "TB"], w=["TB"])
                p.op("dve", lambda e: e.tensor_add(WR[:], WR[:], TB[:]),
                     r=["WR", "TB"], w=["WR"])
                p.op("dve", lambda e, ch=ch, lsl=lsl, s16=s16: e.scalar_tensor_tensor(
                    h[:, ch * LB:][:, lsl], WR[:], scl_s[:, s16:s16 + 1],
                    h[:, ch * LB:][:, lsl], ALU.mult, ALU.add),
                    r=["WR", "cols", "h"], w=["h"])
        ln512(ln2w_s, ln2b_s, lay, False)
        p.dma("spd", lambda e, lay=lay: e.dma_start(
            out=WB[:, 0:8192], in_=w1tp[lay]), r=["WBb"], w=["WBb"])
        p.dma("spd", lambda e, lay=lay: e.dma_start(
            out=WB[:, 8192:16384], in_=w2tp[lay]), r=["WBb"], w=["WBb"])
        for tc in range(8):
            tsl = slice(tc * 512, (tc + 1) * 512)
            for ph in range(2):
                for oc8 in range(8):
                    oc = ph * 8 + oc8
                    bb = oc % 2
                    for kc in range(4):
                        p.op("pe", lambda e, oc=oc, kc=kc, bb=bb, tsl=tsl: e.matmul(
                            ps[:, bb * 512:(bb + 1) * 512],
                            WB[:, kc * 2048 + oc * 128:kc * 2048 + (oc + 1) * 128],
                            ybf[:, kc * LB:][:, tsl],
                            start=(kc == 0), stop=(kc == 3)),
                            r=["WBb", "ybf"], w=[f"FB{bb}"])
                    p.op("act", lambda e, oc=oc, oc8=oc8, bb=bb, lay=lay:
                         e.activation(
                             fbuf[:, oc8 * 512:(oc8 + 1) * 512],
                             ps[:, bb * 512:(bb + 1) * 512], AF.Silu,
                             bias=b1c_s[:, lay * 16 + oc:lay * 16 + oc + 1]),
                         r=[f"FB{bb}", "cols"], w=["fbuf"])
                for dd in range(4):
                    for kc8 in range(8):
                        kc = ph * 8 + kc8
                        p.op("pe", lambda e, dd=dd, kc=kc, kc8=kc8, ph=ph:
                             e.matmul(
                                 ps[:, 1024 + dd * 512:1024 + (dd + 1) * 512],
                                 WB[:, 8192 + kc * 512 + dd * 128:
                                    8192 + kc * 512 + (dd + 1) * 128],
                                 fbuf[:, kc8 * 512:(kc8 + 1) * 512],
                                 start=(ph == 0 and kc8 == 0), stop=False),
                                 r=["WBb", "fbuf"], w=[f"HB{dd}"])
            for dd in range(4):
                p.op("pe", lambda e, dd=dd, lay=lay: e.matmul(
                    ps[:, 1024 + dd * 512:1024 + (dd + 1) * 512],
                    b2r_s[:, (lay * 4 + dd) * 128:(lay * 4 + dd + 1) * 128],
                    ones512[:], start=False, stop=True),
                    r=["cols", "ones512"], w=[f"HB{dd}"])
                p.op("dve", lambda e, dd=dd, tsl=tsl: e.scalar_tensor_tensor(
                    h[:, dd * LB:][:, tsl],
                    ps[:, 1024 + dd * 512:1024 + (dd + 1) * 512], 1.0,
                    h[:, dd * LB:][:, tsl], ALU.mult, ALU.add),
                    r=[f"HB{dd}", "h"], w=["h"])

    def hosl(ch, tsl=None):
        base = h[:, ch * LB:ch * LB + NT]
        return base if tsl is None else h[:, ch * LB:ch * LB + NT][:, tsl]
    for ch in range(4):
        p.op("dve", lambda e, ch=ch: e.tensor_scalar(
            hosl(ch), h[:, ch * LB:ch * LB + NT],
            ownq_s[:, 0:1], None, ALU.mult), r=["h", "cols"], w=["h", "ho"])
        for q in range(1, 4):
            p.op("dve", lambda e, ch=ch, q=q: e.scalar_tensor_tensor(
                hosl(ch), h[:, ch * LB + q * NT:ch * LB + (q + 1) * NT],
                ownq_s[:, q:q + 1], hosl(ch),
                ALU.mult, ALU.add), r=["h", "cols", "ho"], w=["h", "ho"])

    for tc in range(2):
        tsl = slice(tc * 512, (tc + 1) * 512)
        for ch in range(4):
            p.op("dve", lambda e, ch=ch, tsl=tsl: e.tensor_copy(
                hbf4[:, ch * 512:(ch + 1) * 512], hosl(ch, tsl)),
                r=["ho"], w=["hbf4"])
        p.op("act", lambda e: e.activation(sqb4[:], hbf4[:], AF.Square),
             r=["hbf4"], w=["sqb4"])
        for ch in range(4):
            p.op("pe", lambda e, ch=ch: e.matmul(
                ps[:, 0:512], ones_bf[:], hbf4[:, ch * 512:(ch + 1) * 512],
                start=(ch == 0), stop=(ch == 3)), r=["hbf4", "onesA"], w=["B0"])
            p.op("pe", lambda e, ch=ch: e.matmul(
                ps[:, 512:1024], ones_bf[:], sqb4[:, ch * 512:(ch + 1) * 512],
                start=(ch == 0), stop=(ch == 3)), r=["sqb4", "onesA"], w=["B1"])
        p.op("act", lambda e: e.activation(mu4[:], ps[:, 0:512], AF.Copy,
                                           scale=1.0 / 512), r=["B0"], w=["mu4"])
        p.op("act", lambda e: e.activation(rs4[:], ps[:, 512:1024], AF.Copy,
                                           scale=1.0 / 512), r=["B1"], w=["rs4"])
        p.op("dve", lambda e: e.tensor_mul(TA[:, 0:512], mu4[:], mu4[:]),
             r=["mu4"], w=["TA"])
        p.op("dve", lambda e: e.tensor_sub(rs4[:], rs4[:], TA[:, 0:512]),
             r=["rs4", "TA"], w=["rs4"])
        p.op("act", lambda e: e.activation(rs4[:], rs4[:], AF.Ln, bias=EPS),
             r=["rs4", "consts"], w=["rs4"])
        p.op("act", lambda e: e.activation(rs4[:], rs4[:], AF.Exp, scale=-0.5),
             r=["rs4"], w=["rs4"])
        for ch in range(4):
            p.op("dve", lambda e, ch=ch, tsl=tsl: e.tensor_sub(
                TB[:, 0:512], hosl(ch, tsl), mu4[:]),
                r=["ho", "mu4", "TB"], w=["TB"])
            p.op("dve", lambda e: e.tensor_mul(TB[:, 0:512], TB[:, 0:512], rs4[:]),
                 r=["TB", "rs4"], w=["TB"])
            p.op("dve", lambda e, ch=ch, tsl=tsl: e.tensor_scalar(
                ybf[:, ch * NT:][:, tsl], TB[:, 0:512],
                lnlw_s[:, ch:ch + 1], lnlb_s[:, ch:ch + 1], ALU.mult, ALU.add),
                r=["TB", "cols"], w=["ybf"])

    p.dma("spd", lambda e: e.dma_start(out=WB[:], in_=woutp), r=["WBb"], w=["WBb"])
    xin_t = ctx.sbuf("xin_tA", [128, 2 * 512], F32)

    def emit_pre(oc, th, dst, dres):
        bb = th
        psl = ps[:, 1024 + bb * 512:1024 + (bb + 1) * 512]
        for kc in range(4):
            p.op("pe", lambda e, oc=oc, th=th, kc=kc, psl=psl: e.matmul(
                psl, WB[:, kc * DM + oc * 128:kc * DM + (oc + 1) * 128],
                ybf[:, kc * NT + th * 512:kc * NT + (th + 1) * 512],
                start=(kc == 0), stop=(kc == 3)),
                r=["WBb", "ybf"], w=[f"HB{bb}"])
        xsl = xin_t[:, th * 512:(th + 1) * 512]
        p.dma("sp", lambda e, oc=oc, th=th, xsl=xsl: e.dma_start(
            out=xsl, in_=xown[oc * 128:(oc + 1) * 128, th * 512:(th + 1) * 512]),
            w=[f"xinA{th}"])
        p.op("dve", lambda e, psl=psl, xsl=xsl, dst=dst: e.scalar_tensor_tensor(
            dst, psl, 1.0, xsl, ALU.mult, ALU.add),
            r=[f"HB{bb}", f"xinA{th}"], w=[dres])
        p.op("dve", lambda e, oc=oc, dst=dst: e.tensor_scalar(
            dst, dst, boutc_s[:, oc:oc + 1], None, ALU.add),
            r=["cols", dres], w=[dres])

    fm_stage_ln(ctx, "aln", 512, 2, xout, emit_pre,
                mean_ps=ps[:, 0:1024], e2_ps=ps[:, 2048:3072],
                mres="B0", eres="HB2e",
                bufs=(TA, CBt, SBt, WR, UR, TB))


def build_A():
    ctx = Ctx()
    ctx.load_consts()
    pool = ctx.psum("pool", [128, 3584], F32)
    xbat = ctx.din("xbat", [DM, 4096])
    xown = ctx.din("xown", [DM, NT])
    ownq = ctx.din("ownq", [128, 4])
    xout = ctx.dout("xout", [DM, NT])
    emit_A(ctx, pool, xbat, xown, ownq, xout)
    return ctx.build(["xout"])


def host_prep_A(inp):
    def cols16(a):
        return np.ascontiguousarray(
            np.asarray(a).reshape(4, 4, 128).transpose(2, 0, 1).reshape(128, 16))
    WinT = np.asarray(inp["L0_attn_Win"]).T[PI]
    winp = np.ascontiguousarray(WinT.reshape(32, 128, 512).transpose(1, 0, 2))
    w1tp = np.ascontiguousarray(
        np.asarray(inp["L0_attn_W1"]).transpose(0, 2, 1)
        .reshape(4, 4, 128, 2048).transpose(0, 2, 1, 3).reshape(4, 128, 4 * 2048))
    w2tp = np.ascontiguousarray(
        np.asarray(inp["L0_attn_W2"]).transpose(0, 2, 1)
        .reshape(4, 16, 128, 512).transpose(0, 2, 1, 3).reshape(4, 128, 16 * 512))
    woutp = np.ascontiguousarray(
        np.asarray(inp["L0_attn_Wout"])[PI].T.reshape(4, 128, DM)
        .transpose(1, 0, 2).reshape(128, 4 * DM))
    d = {
        "winp": winp, "w1tp": w1tp, "w2tp": w2tp, "woutp": woutp,
        "ln1w": cols16(inp["L0_attn_ln1_w"]), "ln1b": cols16(inp["L0_attn_ln1_b"]),
        "ln2w": cols16(inp["L0_attn_ln2_w"]), "ln2b": cols16(inp["L0_attn_ln2_b"]),
        "sclc": cols16(inp["L0_attn_sc_lin"]),
        "phre": cols16(inp["L0_attn_phz_re"]), "phim": cols16(inp["L0_attn_phz_im"]),
        "lcre": cols16(inp["L0_attn_lc_re"]), "lcim": cols16(inp["L0_attn_lc_im"]),
        "binc": np.ascontiguousarray(
            np.asarray(inp["L0_attn_bin"]).reshape(4, 128).T),
        "b1c": np.ascontiguousarray(
            np.asarray(inp["L0_attn_b1"]).reshape(4, 16, 128)
            .transpose(2, 0, 1).reshape(128, 64)),
        "b2r": np.ascontiguousarray(
            np.asarray(inp["L0_attn_b2"]).reshape(1, 2048)),
        "lnlw": np.ascontiguousarray(
            np.asarray(inp["L0_attn_lnl_w"]).reshape(4, 128).T),
        "lnlb": np.ascontiguousarray(
            np.asarray(inp["L0_attn_lnl_b"]).reshape(4, 128).T),
        "boutc": np.ascontiguousarray(
            np.asarray(inp["L0_attn_bout"])[PI].reshape(32, 128).T),
        "identd": np.eye(128, dtype=np.float32),
        "consts_in": CONSTS_HOST,
    }
    return d



def build_fused():
    ctx = Ctx()
    p = ctx.p
    ctx.load_consts()
    identd = ctx.din("identd", [128, 128], F32)
    ident = ctx.sbuf("identG", [128, 128], F32)
    p.dma("spd", lambda e: e.dma_start(out=ident[:], in_=identd), w=["ident"])
    pool = ctx.psum("pool", [128, 3584], F32)
    poolb = ctx.psum("poolb", [128, 1024], BF16)

    xbat = ctx.din("xbat", [DM, 4096])
    xown = ctx.din("xown", [DM, NT])
    ownq = ctx.din("ownq", [128, 4])
    out_tm = ctx.dout("out_tm", [NT, DM])
    x1d = ctx.dram("x1d", [DM, NT])
    x2d = ctx.dram("x2d", [DM, NT])
    x3d = ctx.dram("x3d", [DM, NT])

    with ctx.scope() as sc:
        ctx.cur_scope = sc
        emit_A(ctx, pool, xbat, xown, ownq, x1d)
        ctx.cur_scope = None
    p.barrier()
    with ctx.scope() as sc:
        ctx.cur_scope = sc
        emit_B(ctx, pool, x1d, x2d, ident)
        ctx.cur_scope = None
    p.barrier()
    with ctx.scope() as sc:
        ctx.cur_scope = sc
        emit_C(ctx, pool, poolb, x2d, x3d, ident)
        ctx.cur_scope = None
    p.barrier()
    with ctx.scope() as sc:
        ctx.cur_scope = sc
        emit_D(ctx, pool, x3d, out_tm, ident=ident)
        ctx.cur_scope = None
    return ctx.build(["xout"])


def input_names(nc):
    out = set()
    for alloc in nc.m.functions[0].allocations:
        try:
            if alloc.kind == "ExternalInput":
                out.add(alloc.memorylocations[0].name)
        except AttributeError:
            pass
    return out


_NC_CACHE = {}


def _get_nc(name):
    if name not in _NC_CACHE:
        _NC_CACHE[name] = {"A": build_A, "B": build_B, "C": build_C,
                           "D": build_D, "F": build_fused}[name]()
    return _NC_CACHE[name]


def kernel(**inp):
    x = np.asarray(inp["L0_attn_x"], dtype=np.float32).reshape(8192, DM)
    packs = {}
    packs.update(host_prep_A(inp))
    packs.update(host_prep_B(inp))
    packs.update(host_prep_C(inp))
    packs.update(host_prep_D(inp))
    packs["identd"] = np.eye(128, dtype=np.float32)
    packs["consts_in"] = CONSTS_HOST

    ncF = _get_nc("F")
    names = input_names(ncF)
    xbat = [np.ascontiguousarray(x[b * 4096:(b + 1) * 4096][:, PI].T)
            for b in range(2)]
    in_maps = []
    for c in range(8):
        q = c % 4
        ownq = np.zeros((128, 4), np.float32)
        ownq[:, q] = 1.0
        m = {"xbat": xbat[c // 4],
             "xown": np.ascontiguousarray(xbat[c // 4][:, q * NT:(q + 1) * NT]),
             "ownq": ownq}
        m.update({k: v for k, v in packs.items() if k in names})
        assert set(m) >= (names - {"partition_id"}), sorted(names - set(m))
        in_maps.append(m)
    res = run_bass_kernel_spmd(ncF, in_maps, core_ids=list(range(8)))

    out = np.empty((8192, DM), dtype=np.float32)
    for c in range(8):
        t0 = (c // 4) * 4096 + (c % 4) * NT
        out[t0:t0 + NT] = np.asarray(res.results[c]["out_tm"])[:, PIINV]
    return out.reshape(2, 4096, DM)


# version 08 (final)
# speedup vs baseline: 1.2119x; 1.2119x over previous
import numpy as np
import ml_dtypes
from collections import defaultdict
from contextlib import ExitStack

import concourse.bass as bass
import concourse.mybir as mybir
from concourse.bass import IndirectOffsetOnAxis
from concourse.bass_utils import run_bass_kernel_spmd

F32 = mybir.dt.float32
BF16 = mybir.dt.bfloat16
U32 = mybir.dt.uint32
AF = mybir.ActivationFunctionType
ALU = mybir.AluOpType
AX = mybir.AxisListType

DM = 4096
NT = 1024
LSEQ = 4096
NCORES = 8
EPS = 1e-5
MAGIC = 1.5 * 2.0 ** 23
PIHALF = float(np.pi / 2)
TWO_PI = float(2 * np.pi)
INV_2PI = float(1.0 / (2 * np.pi))
INV_SQRT128 = float(1.0 / np.sqrt(128.0))

_f = np.arange(DM)
PI = (_f % 128) * 32 + (_f // 128)
PIINV = np.empty(DM, dtype=np.int64)
PIINV[PI] = _f

COMPUTE_STREAMS = ("pe", "act", "dve", "pool")
DMA_STREAMS = ("sp", "spd", "actd")

CONST_NAMES = ["eps", "one", "magic", "pihalf", "neg2pi", "inv2pi", "magicneg", "pi"]
CONST_PYVALS = [EPS, 1.0, MAGIC, PIHALF, -TWO_PI, INV_2PI, -MAGIC, float(np.pi)]
CONSTS_HOST = np.ascontiguousarray(
    np.tile(np.array(CONST_PYVALS, dtype=np.float32), (128, 1)))


class Prog:
    def __init__(self, nc):
        self.nc = nc
        self.last_write = {}
        self.readers = defaultdict(list)
        self.counts = defaultdict(int)
        self._records = []

    def _dep(self, stream, res, write):
        deps = []
        lw = self.last_write.get(res)
        if lw is not None:
            deps.append(lw)
        if write:
            deps.extend(self.readers.get(res, ()))
        return deps

    def op(self, stream, fn, r=(), w=()):
        deps = []
        for res in r:
            deps.extend(self._dep(stream, res, False))
        for res in w:
            deps.extend(self._dep(stream, res, True))
        deps = [(t, q) for (t, q) in deps
                if not (t == stream and q > self.counts[stream])]
        self.counts[stream] += 1
        seq = self.counts[stream]
        self._records.append((stream, fn, deps))
        for res in r:
            self.readers[res].append((stream, seq))
        for res in w:
            self.last_write[res] = (stream, seq)
            self.readers[res] = []
        return seq

    dma = op

    def barrier(self):
        snap = {t: self.counts[t] for t in self.counts if self.counts[t] > 0}
        streams = set(list(COMPUTE_STREAMS) + list(DMA_STREAMS))
        for st in streams:
            deps = [(t, q) for t, q in snap.items() if t != st]
            self._records.append((st, None, deps))

    def sync_point(self, stream, r=()):
        deps = []
        for res in r:
            deps.extend(self._dep(stream, res, False))
        self._records.append((stream, None, deps))

    def finalize(self, block, sems):
        dep_targets = defaultdict(set)
        for stream, fn, deps in self._records:
            for (t, q) in deps:
                dep_targets[t].add(q)
        targets = defaultdict(set)
        for s in dep_targets:
            targets[s] |= dep_targets[s]
        for s, n in self.counts.items():
            if n > 0:
                targets[s].add(n)
            if s in DMA_STREAMS:
                targets[s].update(range(1, n + 1))
        eng_section = {
            "pe": "tensor", "act": "scalar", "dve": "vector",
            "pool": "gpsimd", "sp": "sync", "spd": "gpsimd", "actd": "scalar",
        }
        by_section = defaultdict(list)
        seqno = defaultdict(int)
        for stream, fn, deps in self._records:
            if fn is not None:
                seqno[stream] += 1
            by_section[eng_section[stream]].append((stream, seqno[stream], fn, deps))
        observed = defaultdict(lambda: defaultdict(int))

        def make_body(section, items):
            def body(eng):
                pending = defaultdict(int)
                section_streams = {s for s, _, _, _ in items}
                for stream, seq, fn, deps in items:
                    need = defaultdict(int)
                    for (t, q) in deps:
                        need[t] = max(need[t], q)
                    for t in sorted(need):
                        q = need[t]
                        if observed[stream][t] >= q:
                            continue
                        mult = 16 if t in DMA_STREAMS else 1
                        eng.wait_ge(sems[t], q * mult)
                        for s2 in section_streams:
                            observed[s2][t] = max(observed[s2][t], q)
                    if fn is None:
                        continue
                    inst = fn(eng)
                    mult = 16 if stream in DMA_STREAMS else 1
                    pending[stream] += mult
                    if seq in targets[stream]:
                        inst.then_inc(sems[stream], pending[stream])
                        pending[stream] = 0
                    if stream in DMA_STREAMS and seq in dep_targets[stream]:
                        eng.wait_ge(sems[stream], seq * 16)
                        observed[stream][stream] = max(
                            observed[stream][stream], seq)
            return body

        for section in ("sync", "tensor", "scalar", "vector", "gpsimd"):
            if section in by_section:
                getattr(block, section)(make_body(section, by_section[section]))


class Ctx:
    def __init__(self):
        self.nc = bass.Bass("TRN2", target_bir_lowering=False)
        self.p = Prog(self.nc)
        self.stack = ExitStack()

    def _uniq(self, name):
        if not hasattr(self, "_names"):
            self._names = set()
        base, i = name, 1
        while name in self._names:
            i += 1
            name = f"{base}_{i}"
        self._names.add(name)
        return name

    def sbuf(self, name, shape, dtype):
        st = getattr(self, "cur_scope", None) or self.stack
        return st.enter_context(
            self.nc.sbuf_tensor(self._uniq(name), list(shape), dtype))

    def scope(self):
        from contextlib import ExitStack as _ES
        return _ES()

    def psum(self, name, shape, dtype=F32):
        return self.stack.enter_context(self.nc.psum_tensor(name, list(shape), dtype))

    def din(self, name, shape, dtype=F32):
        return self.nc.dram_tensor(name, list(shape), dtype, kind="ExternalInput").ap()

    def dout(self, name, shape, dtype=F32):
        return self.nc.dram_tensor(name, list(shape), dtype, kind="ExternalOutput").ap()

    def dram(self, name, shape, dtype=F32):
        return self.nc.dram_tensor(self._uniq(name), list(shape), dtype,
                                   kind="Internal").ap()

    def load_consts(self):
        self.consts = self.sbuf("consts_sb", [128, len(CONST_NAMES)], F32)
        cd = self.din("consts_in", [128, len(CONST_NAMES)])
        self.p.dma("sp", lambda e: e.dma_start(out=self.consts[:], in_=cd),
                   w=["consts"])
        for i, v in enumerate(CONST_PYVALS):
            self.nc.const_aps.register(F32, v, self.consts[:, i:i + 1])

    def ccol(self, name):
        i = CONST_NAMES.index(name)
        return self.consts[:, i:i + 1]

    def build(self, out_resources):
        for s in DMA_STREAMS:
            self.p.sync_point(s, r=out_resources)
        used = set()
        for stream, fn, deps in self.p._records:
            if fn is not None:
                used.add(stream)
            for (t, _) in deps:
                used.add(t)
        sems = {}
        for s in sorted(used):
            sems[s] = self.stack.enter_context(self.nc.semaphore("sem_" + s))
        block = self.stack.enter_context(self.nc.Block())
        self.p.finalize(block, sems)
        self.stack.close()
        return self.nc



def fm_stage_ln(ctx, tag, TH, nth, xout, emit_pre, mean_ps=None, e2_ps=None,
                mres=None, eres=None, bufs=None):
    nc, p = ctx.nc, ctx.p
    assert nth == 2
    ones_bf = ctx.sbuf(tag + "_ones", [128, 128], BF16)
    p.op("dve", lambda e: e.memset(ones_bf[:], 1.0), w=[tag + "ones"])
    if bufs is None:
        pre_sb = ctx.sbuf(tag + "_pre", [128, 2 * TH], F32)
        pre_bf = ctx.sbuf(tag + "_prebf", [128, 2 * TH], BF16)
        sq_bf = ctx.sbuf(tag + "_sqbf", [128, 2 * TH], BF16)
        mu_b = ctx.sbuf(tag + "_mu", [128, 2 * TH], F32)
        e2_b = ctx.sbuf(tag + "_e2", [128, 2 * TH], F32)
        rstd_b = ctx.sbuf(tag + "_rstd", [128, 2 * TH], F32)
    else:
        pre_sb, pre_bf, sq_bf, mu_b, e2_b, rstd_b = bufs
    spill = ctx.dram(tag + "_spill", [32 * 128, TH * nth], F32)
    if mean_ps is None:
        mean_ps = ctx.psum(tag + "_mps", [128, 2 * TH], F32)
        e2_ps = ctx.psum(tag + "_eps", [128, 2 * TH], F32)
    mres = mres or tag + "mps"
    eres = eres or tag + "eps" 

    for oc in range(32):
        for th in range(nth):
            bi = th
            dst = pre_sb[:, bi * TH:(bi + 1) * TH]
            emit_pre(oc, th, dst, tag + f"pre{bi}")
            p.op("dve", lambda e, dst=dst, bi=bi: e.tensor_copy(
                pre_bf[:, bi * TH:(bi + 1) * TH], dst),
                r=[tag + f"pre{bi}"], w=[tag + f"prebf{bi}"])
            p.op("act", lambda e, bi=bi: e.activation(
                sq_bf[:, bi * TH:(bi + 1) * TH],
                pre_bf[:, bi * TH:(bi + 1) * TH], AF.Square),
                r=[tag + f"prebf{bi}"], w=[tag + f"sqbf{bi}"])
            p.op("pe", lambda e, bi=bi, oc=oc: e.matmul(
                mean_ps[:, bi * TH:(bi + 1) * TH], ones_bf[:],
                pre_bf[:, bi * TH:(bi + 1) * TH],
                start=(oc == 0), stop=(oc == 31)),
                r=[tag + f"prebf{bi}", tag + "ones"], w=[mres])
            p.op("pe", lambda e, bi=bi, oc=oc: e.matmul(
                e2_ps[:, bi * TH:(bi + 1) * TH], ones_bf[:],
                sq_bf[:, bi * TH:(bi + 1) * TH],
                start=(oc == 0), stop=(oc == 31)),
                r=[tag + f"sqbf{bi}", tag + "ones"], w=[eres])
            p.dma("sp", lambda e, dst=dst, oc=oc, th=th: e.dma_start(
                out=spill[oc * 128:(oc + 1) * 128, th * TH:(th + 1) * TH],
                in_=dst),
                r=[tag + f"pre{bi}"], w=[tag + "spill"])
    p.op("act", lambda e: e.activation(mu_b[:], mean_ps[:], AF.Copy,
                                       scale=1.0 / DM),
         r=[mres], w=[tag + "mu"])
    p.op("act", lambda e: e.activation(e2_b[:], e2_ps[:], AF.Copy,
                                       scale=1.0 / DM),
         r=[eres], w=[tag + "e2"])
    p.op("dve", lambda e: e.tensor_mul(rstd_b[:], mu_b[:], mu_b[:]),
         r=[tag + "mu"], w=[tag + "rstd"])
    p.op("dve", lambda e: e.tensor_sub(e2_b[:], e2_b[:], rstd_b[:]),
         r=[tag + "rstd", tag + "e2"], w=[tag + "e2"])
    p.op("act", lambda e: e.activation(e2_b[:], e2_b[:], AF.Ln, bias=EPS),
         r=[tag + "e2", "consts"], w=[tag + "e2"])
    p.op("act", lambda e: e.activation(rstd_b[:], e2_b[:], AF.Exp, scale=-0.5),
         r=[tag + "e2"], w=[tag + "rstd"])
    for oc in range(32):
        for th in range(nth):
            bi = th
            dst = pre_sb[:, bi * TH:(bi + 1) * TH]
            p.dma("sp", lambda e, dst=dst, oc=oc, th=th: e.dma_start(
                out=dst,
                in_=spill[oc * 128:(oc + 1) * 128, th * TH:(th + 1) * TH]),
                r=[tag + "spill", tag + f"pre{bi}", tag + f"prebf{bi}"],
                w=[tag + f"pre{bi}"])
            p.op("dve", lambda e, dst=dst, bi=bi: e.tensor_sub(
                dst, dst, mu_b[:, bi * TH:(bi + 1) * TH]),
                r=[tag + f"pre{bi}", tag + "mu"], w=[tag + f"pre{bi}"])
            p.op("dve", lambda e, dst=dst, bi=bi: e.tensor_mul(
                dst, dst, rstd_b[:, bi * TH:(bi + 1) * TH]),
                r=[tag + f"pre{bi}", tag + "rstd"], w=[tag + f"pre{bi}"])
            p.dma("sp", lambda e, dst=dst, oc=oc, th=th: e.dma_start(
                out=xout[oc * 128:(oc + 1) * 128, th * TH:(th + 1) * TH],
                in_=dst),
                r=[tag + f"pre{bi}"], w=["xout"])



def emit_B(ctx, pool, xin, xout, ident, dbg=False):
    nc, p = ctx.nc, ctx.p
    wb = ctx.din("wb", [32, 128, 32, 128], BF16)
    gw = ctx.din("gw", [128, 32, 8])
    aT = ctx.din("aT", [128, 32, 64])
    ber = ctx.din("ber", [64, DM])
    r64 = ctx.din("r64", [8, 64])

    xb = ctx.sbuf("xb", [128, 32 * NT], BF16)
    for ic in range(32):
        p.dma("spd", lambda e, ic=ic: e.dma_start(
            out=xb[:, ic * NT:(ic + 1) * NT], in_=xin[ic * 128:(ic + 1) * 128, :]),
            w=["xb"])

    gwt = ctx.sbuf("gwt", [128, 32 * 8], BF16)
    p.dma("spd", lambda e: e.dma_start(out=gwt[:], in_=gw.rearrange("p a b -> p (a b)")),
          w=["gwt"])
    ps_big = pool[:, 0:NT]
    g_ps = ps_big[0:8, :]
    for ic in range(32):
        p.op("pe", lambda e, ic=ic: e.matmul(
            g_ps[:, 0:512], gwt[:, ic * 8:(ic + 1) * 8],
            xb[:, ic * NT:ic * NT + 512], start=(ic == 0), stop=(ic == 31)),
            r=["gwt", "xb"], w=["PBA"])
        p.op("pe", lambda e, ic=ic: e.matmul(
            g_ps[:, 512:1024], gwt[:, ic * 8:(ic + 1) * 8],
            xb[:, ic * NT + 512:ic * NT + 1024], start=(ic == 0), stop=(ic == 31)),
            r=["gwt", "xb"], w=["PBB"])
    g_sb = ctx.sbuf("g_sb", [8, NT], F32)
    p.op("act", lambda e: e.activation(g_sb[:, 0:512], g_ps[:, 0:512], AF.Copy),
         r=["PBA"], w=["g_sb"])
    p.op("act", lambda e: e.activation(g_sb[:, 512:1024], g_ps[:, 512:1024], AF.Copy),
         r=["PBB"], w=["g_sb"])


    gt_ps = pool[:, 1024:1152]
    g_tm = ctx.sbuf("g_tm", [128, 8 * 8], F32)
    for tt in range(8):
        p.op("pe", lambda e, tt=tt: e.transpose(
            gt_ps[:, 0:8], g_sb[:, tt * 128:(tt + 1) * 128], ident[0:8, 0:8]),
            r=["g_sb", "ident"], w=["gt_ps"])
        p.op("act", lambda e, tt=tt: e.activation(
            g_tm[:, tt * 8:(tt + 1) * 8], gt_ps[:, 0:8], AF.Copy),
            r=["gt_ps"], w=["g_tm"])

    m1 = ctx.sbuf("m1", [128, 8], F32)
    m2 = ctx.sbuf("m2", [128, 8], F32)
    w2e = ctx.sbuf("w2e", [128, 8], F32)
    winv = ctx.sbuf("winv", [128, 8], F32)
    gtmp = ctx.sbuf("gtmp", [128, 8 * 8], F32)
    eq1 = ctx.sbuf("eq1", [128, 8 * 8], F32)
    wd = ctx.sbuf("wd", [128, 8 * 8], F32)
    for tt in range(8):
        sl = slice(tt * 8, (tt + 1) * 8)
        gtile = g_tm[:, sl]
        p.op("dve", lambda e, gtile=gtile, tt=tt: e.reduce_max(
            m1[:, tt:tt + 1], gtile, AX.X), r=["g_tm"], w=["m1"])
        p.op("dve", lambda e, gtile=gtile, tt=tt, sl=sl: e.tensor_scalar(
            eq1[:, sl], gtile, m1[:, tt:tt + 1], None, ALU.is_equal),
            r=["g_tm", "m1"], w=["eq1"])
        p.op("dve", lambda e, gtile=gtile, tt=tt, sl=sl: e.scalar_tensor_tensor(
            gtmp[:, sl], eq1[:, sl], -1e30, gtile, ALU.mult, ALU.add),
            r=["g_tm", "eq1"], w=["gtmp"])
        p.op("dve", lambda e, tt=tt, sl=sl: e.reduce_max(
            m2[:, tt:tt + 1], gtmp[:, sl], AX.X), r=["gtmp"], w=["m2"])
    p.op("dve", lambda e: e.tensor_sub(w2e[:], m2[:], m1[:]),
         r=["m1", "m2"], w=["w2e"])
    p.op("act", lambda e: e.activation(w2e[:], w2e[:], AF.Exp),
         r=["w2e"], w=["w2e"])
    p.op("act", lambda e: e.activation(winv[:], w2e[:], AF.Copy, bias=1.0),
         r=["w2e", "consts"], w=["winv"])
    p.op("dve", lambda e: e.reciprocal(winv[:], winv[:]), r=["winv"], w=["winv"])
    p.op("act", lambda e: e.activation(winv[:], winv[:], AF.Copy, scale=2.0),
         r=["winv"], w=["winv"])
    p.op("dve", lambda e: e.tensor_mul(w2e[:], w2e[:], winv[:]),
         r=["w2e", "winv"], w=["w2e"])
    for tt in range(8):
        sl = slice(tt * 8, (tt + 1) * 8)
        gtile = g_tm[:, sl]
        p.op("dve", lambda e, gtile=gtile, tt=tt, sl=sl: e.tensor_scalar(
            gtmp[:, sl], gtile, m2[:, tt:tt + 1], None, ALU.is_equal),
            r=["g_tm", "m2"], w=["gtmp"])
        p.op("dve", lambda e, tt=tt, sl=sl: e.tensor_scalar(
            wd[:, sl], eq1[:, sl], winv[:, tt:tt + 1], None, ALU.mult),
            r=["eq1", "winv"], w=["wd"])
        p.op("dve", lambda e, tt=tt, sl=sl: e.scalar_tensor_tensor(
            wd[:, sl], gtmp[:, sl], w2e[:, tt:tt + 1], wd[:, sl],
            ALU.mult, ALU.add), r=["gtmp", "w2e", "wd"], w=["wd"])

    wdf = ctx.sbuf("wdf", [8, NT], F32)
    for tt in range(8):
        p.op("pe", lambda e, tt=tt: e.transpose(
            gt_ps[0:8, 0:128], wd[:, tt * 8:(tt + 1) * 8], ident[:]),
            r=["wd", "ident"], w=["gt_ps"])
        p.op("act", lambda e, tt=tt: e.activation(
            wdf[:, tt * 128:(tt + 1) * 128], gt_ps[0:8, 0:128], AF.Copy),
            r=["gt_ps"], w=["wdf"])
    wdf_bf = ctx.sbuf("wdf_bf", [8, NT], BF16)
    p.op("dve", lambda e: e.tensor_copy(wdf_bf[:], wdf[:]), r=["wdf"], w=["wdf_bf"])
    r64_bf = ctx.sbuf("r64_bf", [8, 64], BF16)
    p.dma("spd", lambda e: e.dma_start(out=r64_bf[:], in_=r64), w=["r64_bf"])
    at_sb = ctx.sbuf("at_sb", [128, 32 * 64], BF16)
    p.dma("spd", lambda e: e.dma_start(out=at_sb[:], in_=aT.rearrange("p a b -> p (a b)")),
          w=["at_sb"])
    u_ps = ps_big[0:64, :]
    for ic in range(32):
        for h in range(2):
            p.op("pe", lambda e, ic=ic, h=h: e.matmul(
                u_ps[:, h * 512:(h + 1) * 512], at_sb[:, ic * 64:(ic + 1) * 64],
                xb[:, ic * NT + h * 512:ic * NT + (h + 1) * 512],
                start=(ic == 0), stop=(ic == 31)),
                r=["at_sb", "xb"], w=["PBA" if h == 0 else "PBB"])
    u_sb = ctx.sbuf("u_sb", [64, NT], F32)
    p.op("act", lambda e: e.activation(u_sb[:], u_ps[:], AF.Copy),
         r=["PBA", "PBB"], w=["u_sb"])
    wrep_ps = ps_big[0:64, :]
    for h in range(2):
        p.op("pe", lambda e, h=h: e.matmul(
            wrep_ps[:, h * 512:(h + 1) * 512], r64_bf[:],
            wdf_bf[:, h * 512:(h + 1) * 512], start=True, stop=True),
            r=["wdf_bf", "r64_bf", "u_sb"], w=["PBA" if h == 0 else "PBB"])
    up_bf = ctx.sbuf("up_bf", [64, NT], BF16)
    p.op("dve", lambda e: e.tensor_mul(up_bf[:], u_sb[:], wrep_ps[:]),
         r=["u_sb", "PBA", "PBB"], w=["up_bf"])

    ber_sb = ctx.sbuf("ber_sb", [64, DM], BF16)
    p.dma("spd", lambda e: e.dma_start(out=ber_sb[:], in_=ber), w=["ber_sb"])

    wtile = ctx.sbuf("wtile", [128, 2 * 4096], BF16)
    out_ps = ps_big
    xin_t = ctx.sbuf("xin_t", [128, 2 * 512], F32)

    def emit_pre(oc, th, dst, dres):
        bi = oc % 2
        wsl = wtile[:, bi * 4096:(bi + 1) * 4096]
        if th == 0:
            p.dma("spd", lambda e, oc=oc, bi=bi: e.dma_start(
                out=wtile[:, bi * 4096:(bi + 1) * 4096],
                in_=wb[oc].rearrange("p a b -> p (a b)")),
                w=[f"wtile{bi}"])
        psl = out_ps[:, th * 512:(th + 1) * 512]
        for ic in range(32):
            p.op("pe", lambda e, ic=ic, oc=oc, th=th, psl=psl, wsl=wsl: e.matmul(
                psl, wsl[:, ic * 128:(ic + 1) * 128],
                xb[:, ic * NT + th * 512:ic * NT + (th + 1) * 512],
                start=(ic == 0), stop=False),
                r=[f"wtile{bi}", "xb"], w=["PBA" if th == 0 else "PBB"])
        p.op("pe", lambda e, oc=oc, th=th, psl=psl: e.matmul(
            psl, ber_sb[:, oc * 128:(oc + 1) * 128],
            up_bf[:, th * 512:(th + 1) * 512], start=False, stop=True),
            r=["ber_sb", "up_bf"], w=["PBA" if th == 0 else "PBB"])
        xsl = xin_t[:, th * 512:(th + 1) * 512]
        p.dma("sp", lambda e, oc=oc, th=th, xsl=xsl: e.dma_start(
            out=xsl, in_=xin[oc * 128:(oc + 1) * 128, th * 512:(th + 1) * 512]),
            w=[f"xin_t{th}"])
        p.op("dve", lambda e, psl=psl, xsl=xsl, dst=dst: e.scalar_tensor_tensor(
            dst, psl, 1.0, xsl, ALU.mult, ALU.add),
            r=["PBA" if th == 0 else "PBB", f"xin_t{th}"], w=[dres])

    fm_stage_ln(ctx, "bln", 512, 2, xout, emit_pre,
                mean_ps=pool[:, 1536:2560], e2_ps=pool[:, 2560:3584],
                mres="BSM", eres="BSE")


def build_B():
    ctx = Ctx()
    ctx.load_consts()
    identd = ctx.din("identd", [128, 128], F32)
    ident = ctx.sbuf("identB", [128, 128], F32)
    ctx.p.dma("spd", lambda e: e.dma_start(out=ident[:], in_=identd), w=["ident"])
    pool = ctx.psum("pool", [128, 3584], F32)
    xin = ctx.din("xin", [DM, NT])
    xout = ctx.dout("xout", [DM, NT])
    emit_B(ctx, pool, xin, xout, ident)
    return ctx.build(["xout"])



def _pack_inchunks(w):
    X = w.shape[1]
    return np.ascontiguousarray(w.reshape(32, 128, X).transpose(1, 0, 2))


def host_prep_B(inp):
    WbT = inp["L0_ffn_W_base"].T[PI][:, PI]
    wb = np.ascontiguousarray(
        WbT.reshape(32, 128, 32, 128).transpose(2, 1, 0, 3)
    ).astype(ml_dtypes.bfloat16)
    gw = _pack_inchunks(inp["L0_ffn_gate_W"].T[PI])
    aT = _pack_inchunks(inp["L0_ffn_A"].reshape(64, DM).T[PI])
    ber = np.ascontiguousarray(
        inp["L0_ffn_B"].transpose(0, 2, 1).reshape(64, DM)[:, PI])
    r64 = np.repeat(np.eye(8, dtype=np.float32), 8, axis=1)
    return {"wb": wb, "gw": gw, "aT": aT, "ber": ber, "r64": r64,
            "identd": np.eye(128, dtype=np.float32),
            "consts_in": CONSTS_HOST}



def emit_D(ctx, pool, xin, out_tm, ident=None, xtm=None, mode="full"):
    nc, p = ctx.nc, ctx.p
    c1p = ctx.din("c1p", [128, 32, 16])
    c2T = ctx.din("c2T", [16, 14336])
    w1p = ctx.din("w1p", [14336, DM], BF16)
    w2p = ctx.din("w2p", [14336, DM], BF16)

    c1sb = ctx.sbuf("c1sb", [128, 32 * 16], BF16)
    p.dma("spd", lambda e: e.dma_start(out=c1sb[:], in_=c1p.rearrange("p a b -> p (a b)")),
          w=["c1sb"])
    c2sb = ctx.sbuf("c2sb", [16, 14336], BF16)
    p.dma("spd", lambda e: e.dma_start(out=c2sb[:], in_=c2T), w=["c2sb"])

    xbt = ctx.sbuf("xbt", [128, 2 * NT], BF16)
    u_ps = pool[0:16, 0:NT]
    for ic in range(32):
        bi = ic % 2
        p.dma("spd", lambda e, ic=ic, bi=bi: e.dma_start(
            out=xbt[:, bi * NT:(bi + 1) * NT],
            in_=xin[ic * 128:(ic + 1) * 128, :]), w=[f"xbt{bi}"])
        for h in range(2):
            p.op("pe", lambda e, ic=ic, bi=bi, h=h: e.matmul(
                u_ps[:, h * 512:(h + 1) * 512], c1sb[:, ic * 16:(ic + 1) * 16],
                xbt[:, bi * NT + h * 512:bi * NT + (h + 1) * 512],
                start=(ic == 0), stop=(ic == 31)),
                r=["c1sb", f"xbt{bi}"], w=["u_ps"])
    u_bf = ctx.sbuf("u_bf", [16, NT], BF16)
    p.op("act", lambda e: e.activation(u_bf[:], u_ps[:], AF.Copy),
         r=["u_ps"], w=["u_bf"])

    iota256 = ctx.sbuf("iota256", [128, 256], F32)
    p.op("pool", lambda e: e.iota(iota256[:], pattern=[[1, 256]], base=0,
                                  channel_multiplier=0,
                                  allow_small_or_imprecise_dtypes=True),
         w=["iota256"])
    kbase = ctx.sbuf("kbase", [128, 56], F32)
    p.op("pool", lambda e: e.iota(kbase[:], pattern=[[256, 56]], base=0,
                                  channel_multiplier=0,
                                  allow_small_or_imprecise_dtypes=True),
         w=["kbase"])

    xt_t = ctx.sbuf("xt_t", [128, 2 * DM], F32)
    xt_bf = ctx.sbuf("xt_bf", [128, 2 * DM], BF16)
    NS1, NS2 = 4, 4
    g1b = ctx.sbuf("g1b", [128, NS1 * DM], BF16)
    g2b = ctx.sbuf("g2b", [128, NS2 * DM], BF16)
    jnkb = ctx.sbuf("jnkb", [128, DM], BF16)
    acc = ctx.sbuf("acc", [128, DM], F32)
    junk = ctx.sbuf("junk", [128, 256], F32)
    outsb = ctx.sbuf("outsb", [128, DM], F32)
    m56 = ctx.sbuf("m56", [128, 56], F32)
    s56 = ctx.sbuf("s56", [128, 2 * 56], F32)
    h56 = ctx.sbuf("h56", [128, 2 * 56], F32)
    idx56 = ctx.sbuf("idx56", [128, 56], F32)
    idxu = ctx.sbuf("idxu", [128, 2 * 56], U32)
    eqb = ctx.sbuf("eqb", [128, 256], F32)
    int_ps = pool[:, 1024:2048]
    colA = ctx.sbuf("colA", [128, 4], F32)

    xfm_t = ctx.sbuf("xfm_t", [128, 128], F32) if xtm is None else None
    for tt in range(8):
        sl = tt % 2
        Rxt, Rxbf, Ridx = f"xt_t{sl}", f"xt_bf{sl}", f"idxu{sl}"
        Rs56, Rh56 = f"s56_{sl}", f"h56_{sl}"
        xsl = xt_t[:, sl * DM:(sl + 1) * DM]
        xbfsl = xt_bf[:, sl * DM:(sl + 1) * DM]
        trows = slice(tt * 128, (tt + 1) * 128)
        if xtm is not None:
            p.dma("sp", lambda e, trows=trows, xsl=xsl: e.dma_start(
                out=xsl, in_=xtm[trows, :]), r=[], w=[Rxt])
        else:
            for mc in range(32):
                p.dma("sp", lambda e, mc=mc, tt=tt: e.dma_start(
                    out=xfm_t[:], in_=xin[mc * 128:(mc + 1) * 128,
                                          tt * 128:(tt + 1) * 128]),
                    r=["xfm_t"], w=["xfm_t"])
                p.op("pe", lambda e, mc=mc: e.transpose(
                    pool[:, 2048:2176], xfm_t[:], ident[:]),
                    r=["xfm_t", "ident", "DTR"], w=["DTR"])
                p.op("act", lambda e, mc=mc, xsl=xsl: e.activation(
                    xsl[:, mc * 128:(mc + 1) * 128], pool[:, 2048:2176],
                    AF.Copy), r=["DTR"], w=[Rxt])
        p.op("dve", lambda e, xbfsl=xbfsl, xsl=xsl: e.tensor_copy(xbfsl, xsl),
             r=[Rxt], w=[Rxbf])
        if mode == "norouter":
            p.op("pool", lambda e, sl=sl: e.iota(
                idxu[:, sl * 56:(sl + 1) * 56], pattern=[[256, 56]], base=0,
                channel_multiplier=0), r=[Ridx], w=[Ridx])
        for kb in range(0 if mode != "norouter" else -1, 56 if mode != "norouter" else -1):
            bi = kb % 2
            psl = int_ps[:, bi * 512:bi * 512 + 256]
            p.op("pe", lambda e, kb=kb, tt=tt, psl=psl: e.matmul(
                psl, u_bf[:, tt * 128:(tt + 1) * 128],
                c2sb[:, kb * 256:(kb + 1) * 256], start=True, stop=True),
                r=["u_bf", "c2sb"], w=[f"IP{bi}"])
            p.op("dve", lambda e, kb=kb, psl=psl: e.reduce_max(
                m56[:, kb:kb + 1], psl, AX.X), r=[f"IP{bi}"], w=["m56"])
            p.op("dve", lambda e, kb=kb, psl=psl: e.tensor_scalar(
                eqb[:], psl, m56[:, kb:kb + 1], None, ALU.is_equal),
                r=[f"IP{bi}", "m56"], w=["eqb"])
            p.op("dve", lambda e, kb=kb: e.scalar_tensor_tensor(
                junk[:, 0:256], iota256[:], 1.0, eqb[:], ALU.mult, ALU.mult,
                accum_out=idx56[:, kb:kb + 1]),
                r=["iota256", "eqb"], w=["junk", "idx56"])
        if mode != "norouter":
            p.op("dve", lambda e: e.tensor_add(idx56[:], idx56[:], kbase[:]),
                 r=["idx56", "kbase"], w=["idx56"])
            p.op("dve", lambda e, sl=sl: e.tensor_copy(
                idxu[:, sl * 56:(sl + 1) * 56], idx56[:]),
                r=["idx56"], w=[Ridx])
        for kb in range(56):
            b = kb % NS1
            gsl = g1b[:, b * DM:(b + 1) * DM]
            if mode == "nogather":
                p.op("dve", lambda e, gsl=gsl: e.memset(gsl, 0.5),
                     r=[f"g1s{b}"], w=[f"g1s{b}"])
            else:
                p.dma("spd", lambda e, kb=kb, sl=sl, gsl=gsl: e.indirect_dma_start(
                    out=gsl, out_offset=None, in_=w1p,
                    in_offset=IndirectOffsetOnAxis(
                        idxu[:, sl * 56 + kb:sl * 56 + kb + 1], 0)),
                    r=[Ridx], w=[f"g1s{b}"])
            p.op("dve", lambda e, kb=kb, sl=sl, gsl=gsl, xbfsl=xbfsl:
                 e.scalar_tensor_tensor(
                     jnkb[:], gsl, 1.0, xbfsl, ALU.mult, ALU.mult,
                     accum_out=s56[:, sl * 56 + kb:sl * 56 + kb + 1]),
                 r=[f"g1s{b}", Rxbf], w=["jnkb", Rs56])
        p.op("dve", lambda e, sl=sl: e.tensor_scalar(
            h56[:, sl * 56:(sl + 1) * 56], s56[:, sl * 56:(sl + 1) * 56],
            0.0, None, ALU.max), r=[Rs56], w=[Rh56])
        p.op("dve", lambda e: e.memset(acc[:], 0.0), w=["acc"])
        for kb in range(56):
            b = kb % NS2
            gsl = g2b[:, b * DM:(b + 1) * DM]
            if mode == "nogather":
                p.op("dve", lambda e, gsl=gsl: e.memset(gsl, 0.5),
                     r=[f"g2s{b}"], w=[f"g2s{b}"])
            else:
                p.dma("spd", lambda e, kb=kb, sl=sl, gsl=gsl: e.indirect_dma_start(
                    out=gsl, out_offset=None, in_=w2p,
                    in_offset=IndirectOffsetOnAxis(
                        idxu[:, sl * 56 + kb:sl * 56 + kb + 1], 0)),
                    r=[Ridx], w=[f"g2s{b}"])
            p.op("dve", lambda e, kb=kb, sl=sl, gsl=gsl: e.scalar_tensor_tensor(
                acc[:], gsl, h56[:, sl * 56 + kb:sl * 56 + kb + 1], acc[:],
                ALU.mult, ALU.add),
                r=[f"g2s{b}", Rh56, "acc"], w=["acc"])
        p.op("dve", lambda e, xsl=xsl: e.tensor_add(acc[:], acc[:], xsl),
             r=["acc", Rxt], w=["acc"])
        p.op("dve", lambda e: e.tensor_reduce(colA[:, 0:1], acc[:], AX.X, ALU.add),
             r=["acc"], w=["colA"])
        p.op("act", lambda e: e.activation(outsb[:], acc[:], AF.Square),
             r=["acc"], w=["outsb"])
        p.op("dve", lambda e: e.tensor_reduce(colA[:, 1:2], outsb[:], AX.X, ALU.add),
             r=["outsb"], w=["colA"])
        p.op("act", lambda e: e.activation(colA[:, 0:1], colA[:, 0:1], AF.Copy,
                                           scale=1.0 / DM), r=["colA"], w=["colA"])
        p.op("act", lambda e: e.activation(colA[:, 1:2], colA[:, 1:2], AF.Copy,
                                           scale=1.0 / DM), r=["colA"], w=["colA"])
        p.op("dve", lambda e: e.tensor_mul(colA[:, 2:3], colA[:, 0:1], colA[:, 0:1]),
             r=["colA"], w=["colA"])
        p.op("dve", lambda e: e.tensor_sub(colA[:, 1:2], colA[:, 1:2], colA[:, 2:3]),
             r=["colA"], w=["colA"])
        p.op("act", lambda e: e.activation(colA[:, 1:2], colA[:, 1:2], AF.Ln,
                                           bias=EPS), r=["colA", "consts"], w=["colA"])
        p.op("act", lambda e: e.activation(colA[:, 1:2], colA[:, 1:2], AF.Exp,
                                           scale=-0.5), r=["colA"], w=["colA"])
        p.op("dve", lambda e: e.tensor_scalar(
            outsb[:], acc[:], colA[:, 0:1], colA[:, 1:2], ALU.subtract, ALU.mult),
            r=["acc", "colA"], w=["outsb"])
        p.dma("sp", lambda e, trows=trows: e.dma_start(
            out=out_tm[trows, :], in_=outsb[:]), r=["outsb"], w=["xout"])


def build_D(mode="full"):
    ctx = Ctx()
    ctx.load_consts()
    pool = ctx.psum("pool", [128, 3584], F32)
    xin = ctx.din("xin", [DM, NT])
    xtm = ctx.din("xtm", [NT, DM])
    out_tm = ctx.dout("out_tm", [NT, DM])
    emit_D(ctx, pool, xin, out_tm, xtm=xtm, mode=mode)
    return ctx.build(["xout"])


def host_prep_D(inp):
    c1p = _pack_inchunks(inp["L1_ffn_c1_w"].T[PI])
    c2T = np.ascontiguousarray(inp["L1_ffn_c2_w"].T)
    w1p = np.ascontiguousarray(
        inp["L1_ffn_w1"][:, PI]).astype(ml_dtypes.bfloat16)
    w2p = np.ascontiguousarray(
        inp["L1_ffn_w2t"][:, PI]).astype(ml_dtypes.bfloat16)
    return {"c1p": c1p, "c2T": c2T, "w1p": w1p, "w2p": w2p,
            "consts_in": CONSTS_HOST}



def emit_C(ctx, pool, poolb, xin, xout, ident):
    nc, p = ctx.nc, ctx.p
    wqp = ctx.din("wqp", [128, 32 * 4096])
    wkp = ctx.din("wkp", [128, 32 * 1024])
    wvp = ctx.din("wvp", [128, 32 * 1024])
    bqp = ctx.din("bqp", [128, 32])
    bkp = ctx.din("bkp", [128, 8])
    bvp = ctx.din("bvp", [128, 8])
    xout = xout
    wqs_d = ctx.dram("wqs_d", [128, 32 * 1024], BF16)

    TT4 = 256
    NSUB = NT // TT4

    ident_bf = ctx.sbuf("ident_bf", [128, 128], BF16)
    p.op("dve", lambda e: e.tensor_copy(ident_bf[:], ident[:]),
         r=["ident"], w=["identbf"])

    wqt = ctx.sbuf("wqt", [128, 4096], BF16)
    wqs_t = ctx.sbuf("wqs_t", [128, 1024], BF16)
    for ic in range(32):
        p.dma("spd", lambda e, ic=ic: e.dma_start(
            out=wqt[:], in_=wqp[:, ic * 4096:(ic + 1) * 4096]),
            r=["wqs_t"], w=["wqt"])
        v3 = wqt[:].rearrange("p (h j d) -> p h j d", h=8, j=4)
        p.op("dve", lambda e, v3=v3: e.tensor_add(
            wqs_t[:].rearrange("p (h d) -> p h d", h=8),
            v3[:, :, 0, :], v3[:, :, 1, :]), r=["wqt"], w=["wqs_t"])
        p.op("dve", lambda e, v3=v3: e.tensor_add(
            wqs_t[:].rearrange("p (h d) -> p h d", h=8),
            wqs_t[:].rearrange("p (h d) -> p h d", h=8), v3[:, :, 2, :]),
            r=["wqt", "wqs_t"], w=["wqs_t"])
        p.op("dve", lambda e, v3=v3: e.tensor_add(
            wqs_t[:].rearrange("p (h d) -> p h d", h=8),
            wqs_t[:].rearrange("p (h d) -> p h d", h=8), v3[:, :, 3, :]),
            r=["wqt", "wqs_t"], w=["wqs_t"])
        p.dma("spd", lambda e, ic=ic: e.dma_start(
            out=wqs_d[:, ic * 1024:(ic + 1) * 1024], in_=wqs_t[:]),
            r=["wqs_t"], w=["wqs_d"])

    bq_sb = ctx.sbuf("bq_sb", [128, 32], F32)
    p.dma("sp", lambda e: e.dma_start(out=bq_sb[:], in_=bqp), w=["bq_sb"])
    bqs = ctx.sbuf("bqs", [128, 8], F32)
    bq3 = bq_sb[:].rearrange("p (h j) -> p h j", h=8)
    p.op("dve", lambda e: e.tensor_add(bqs[:], bq3[:, :, 0], bq3[:, :, 1]),
         r=["bq_sb"], w=["bqs"])
    p.op("dve", lambda e: e.tensor_add(bqs[:], bqs[:], bq3[:, :, 2]),
         r=["bq_sb", "bqs"], w=["bqs"])
    p.op("dve", lambda e: e.tensor_add(bqs[:], bqs[:], bq3[:, :, 3]),
         r=["bq_sb", "bqs"], w=["bqs"])
    bk_sb = ctx.sbuf("bk_sb", [128, 8], F32)
    p.dma("sp", lambda e: e.dma_start(out=bk_sb[:], in_=bkp), w=["bk_sb"])
    bv_sb = ctx.sbuf("bv_sb", [128, 8], F32)
    p.dma("sp", lambda e: e.dma_start(out=bv_sb[:], in_=bvp), w=["bv_sb"])

    q_pad = ctx.sbuf("q_pad", [128, 32 * TT4], BF16)
    k_pad = ctx.sbuf("k_pad", [128, 32 * TT4], BF16)
    v_pad = ctx.sbuf("v_pad", [128, 9 * TT4], BF16)
    p.op("dve", lambda e: e.memset(
        q_pad[:].rearrange("p (t s) -> p t s", s=32)[:, :, 9:32], 0.0), w=["qz"])
    p.op("dve", lambda e: e.memset(
        k_pad[:].rearrange("p (t s) -> p t s", s=32)[:, :, 9:32], 0.0), w=["kz"])
    p.op("dve", lambda e: e.memset(
        k_pad[:].rearrange("p (t s) -> p t s", s=32)[:, :, 8], 1.0), w=["kpad"])
    p.op("dve", lambda e: e.memset(
        v_pad[:].rearrange("p (t s) -> p t s", s=9)[:, :, 8], 1.0), w=["vz"])

    wtile = ctx.sbuf("wtile", [128, 2 * 8192], BF16)
    xall = ctx.sbuf("xall", [128, 32 * TT4], BF16)
    prod = ctx.sbuf("prod", [128, NT * 8], F32)
    sqb = ctx.sbuf("sqb", [128, 8 * TT4], BF16)
    qn = ctx.sbuf("qn", [128, TT4], F32)
    kn2b = ctx.sbuf("kn2b", [128, TT4], BF16)
    kfrow = ctx.sbuf("kfrow", [1, TT4], F32)
    kfs = ctx.sbuf("kfs", [1, 2], F32)
    ones_bf1 = ctx.sbuf("ones_bf1", [128, 1], BF16)
    p.op("dve", lambda e: e.memset(ones_bf1[:], 1.0), w=["ones1"])
    ones_row = ctx.sbuf("ones_row", [1, 128], BF16)
    p.op("dve", lambda e: e.memset(ones_row[:], 1.0), w=["ones1"])
    kfs_bf = ctx.sbuf("kfs_bf", [1, 1], BF16)
    nkf_col = ctx.sbuf("nkf_col", [128, 1], F32)
    qT = ctx.sbuf("qT", [128, 2 * 128], BF16)
    kT = ctx.sbuf("kT", [128, 2 * 128], BF16)
    expT = ctx.sbuf("expT", [128, 2 * 512], BF16)
    og_sb = ctx.sbuf("og_sb", [128, 8 * 9], F32)
    rcp = ctx.sbuf("rcp", [128, TT4], F32)

    ps = pool[:, 0:3072]
    psb = poolb

    def PSL(bank, w=512):
        return ps[:, bank * 512:bank * 512 + w]

    for st in range(NSUB):
        tsl = slice(st * TT4, (st + 1) * TT4)
        for ic in range(32):
            p.dma("spd", lambda e, ic=ic, tsl=tsl: e.dma_start(
                out=xall[:, ic * TT4:(ic + 1) * TT4],
                in_=xin[ic * 128:(ic + 1) * 128, tsl]),
                w=[f"xall{ic}"])
        for proj, wsrc, nhk, pad, bias in (
                ("q", wqs_d, 8, q_pad, bqs),
                ("k", wkp, 8, k_pad, bk_sb),
                ("v", wvp, 8, v_pad, bv_sb)):
            ns = 9 if proj == "v" else 32
            for hkh in range(2):
                for slab in range(4):
                    bi = slab % 2
                    for icr in range(8):
                        ic = slab * 8 + icr
                        p.dma("spd", lambda e, ic=ic, icr=icr, bi=bi, wsrc=wsrc:
                              e.dma_start(
                                  out=wtile[:, bi * 8192 + icr * 1024:
                                            bi * 8192 + (icr + 1) * 1024],
                                  in_=wsrc[:, ic * 1024:(ic + 1) * 1024]),
                              r=(["wqs_d"] if proj == "q" else []),
                              w=[f"wt{bi}_{icr}"])
                    for icr in range(8):
                        ic = slab * 8 + icr
                        for hh in range(4):
                            hk = hkh * 4 + hh
                            p.op("pe", lambda e, ic=ic, icr=icr, bi=bi, hk=hk,
                                 hh=hh: e.matmul(
                                     ps[:, hh * 512:hh * 512 + TT4],
                                     wtile[:, bi * 8192 + icr * 1024 + hk * 128:
                                           bi * 8192 + icr * 1024 + (hk + 1) * 128],
                                     xall[:, ic * TT4:(ic + 1) * TT4],
                                     start=(ic == 0), stop=(ic == 31)),
                                 r=[f"wt{bi}_{icr}", f"xall{ic}"], w=[f"PP{hh}"])
                for hh in range(4):
                    hk = hkh * 4 + hh
                    p.op("dve", lambda e, hk=hk, hh=hh, pad=pad, bias=bias,
                         ns=ns: e.tensor_scalar(
                             pad[:].rearrange("p (t s) -> p t s", s=ns)[:, :, hk],
                             ps[:, hh * 512:hh * 512 + TT4],
                             bias[:, hk:hk + 1], None, ALU.add),
                         r=[f"PP{hh}", "bqs", "bk_sb", "bv_sb"],
                         w=[proj + "pad"])
        p.op("act", lambda e: e.activation(
            sqb[:].rearrange("p (t h) -> p t h", h=8),
            q_pad[:].rearrange("p (t s) -> p t s", s=32)[:, :, 0:8], AF.Square),
            r=["qpad"], w=["sqb"])
        p.op("dve", lambda e: e.tensor_reduce(
            qn[:], sqb[:].rearrange("p (t h) -> p t h", h=8), AX.X, ALU.add),
            r=["sqb"], w=["qn"])
        p.op("act", lambda e: e.activation(qn[:], qn[:], AF.Sqrt),
             r=["qn"], w=["qn"])
        p.op("act", lambda e: e.activation(
            sqb[:].rearrange("p (t h) -> p t h", h=8),
            k_pad[:].rearrange("p (t s) -> p t s", s=32)[:, :, 0:8], AF.Square),
            r=["kpad"], w=["sqb"])
        def _kn2(e):
            with ctx.nc.allow_low_precision(reason="cauchy safety bound only"):
                return e.tensor_reduce(
                    kn2b[:], sqb[:].rearrange("p (t h) -> p t h", h=8),
                    AX.X, ALU.add)
        p.op("dve", _kn2, r=["sqb"], w=["kn2b"])
        p.op("pe", lambda e: e.matmul(ps[0:1, 2816:2816 + TT4], ones_bf1[:],
                                      kn2b[:], start=True, stop=True),
             r=["kn2b", "ones1", "OG"], w=["OG"])
        p.op("act", lambda e: e.activation(kfrow[:], ps[0:1, 2816:2816 + TT4],
                                           AF.Sqrt), r=["OG"], w=["kfrow"])
        p.op("dve", lambda e: e.reduce_max(kfs[:, 0:1], kfrow[:], AX.X),
             r=["kfrow"], w=["kfs"])
        p.op("act", lambda e: e.activation(kfs[:, 1:2], kfs[:, 0:1], AF.Copy,
                                           scale=-1.0), r=["kfs"], w=["kfs"])
        p.op("dve", lambda e: e.tensor_copy(kfs_bf[:], kfs[:, 1:2]),
             r=["kfs"], w=["kfsbf"])
        p.op("pe", lambda e: e.matmul(ps[:, 2688:2689], ones_row[:],
                                      kfs_bf[:], start=True, stop=True),
             r=["kfsbf", "ones1", "OG"], w=["OG"])
        p.op("act", lambda e: e.activation(nkf_col[:], ps[:, 2688:2689],
                                           AF.Copy), r=["OG"], w=["nkf"])
        p.op("dve", lambda e: e.tensor_scalar(
            q_pad[:].rearrange("p (t s) -> p t s", s=32)[:, :, 8],
            qn[:], nkf_col[:], None, ALU.mult),
            r=["qn", "nkf"], w=["qpad"])
        for g in range(TT4 // 4):
            gb = g % 2
            qv = q_pad[:, g * 128:(g + 1) * 128]
            kv = k_pad[:, g * 128:(g + 1) * 128]
            p.op("pe", lambda e, qv=qv, gb=gb: e.transpose(
                psb[:, gb * 512:gb * 512 + 128], qv, ident_bf[:]),
                r=["qpad", "qz", "identbf"], w=[f"PSB{gb}"])
            p.op("act", lambda e, gb=gb: e.activation(
                qT[:, gb * 128:(gb + 1) * 128], psb[:, gb * 512:gb * 512 + 128],
                AF.Copy), r=[f"PSB{gb}"], w=[f"qT{gb}"])
            p.op("pe", lambda e, kv=kv, gb=gb: e.transpose(
                psb[:, gb * 512 + 128:gb * 512 + 256], kv, ident_bf[:]),
                r=["kpad", "kz", "identbf"], w=[f"PSB{gb}"])
            p.op("act", lambda e, gb=gb: e.activation(
                kT[:, gb * 128:(gb + 1) * 128],
                psb[:, gb * 512 + 128:gb * 512 + 256],
                AF.Copy), r=[f"PSB{gb}"], w=[f"kT{gb}"])
            for j in range(4):
                p.op("pe", lambda e, j=j, gb=gb: e.matmul(
                    ps[:, 2048 + j * 128:2048 + (j + 1) * 128],
                    kT[32 * j:32 * j + 32, gb * 128:(gb + 1) * 128],
                    qT[32 * j:32 * j + 32, gb * 128:(gb + 1) * 128],
                    start=True, stop=True, tile_position=(32 * j, 0)),
                    r=[f"kT{gb}", f"qT{gb}", "SC"], w=["SC"])
            p.op("act", lambda e, gb=gb: e.activation(
                expT[:, gb * 512:(gb + 1) * 512],
                ps[:, 2048:2560],
                AF.Exp, scale=INV_SQRT128), r=["SC"], w=[f"expT{gb}"])
            for j in range(4):
                t = 4 * g + j
                vv = v_pad[:, t * 9:(t + 1) * 9]
                p.op("pe", lambda e, j=j, g=g, gb=gb, vv=vv: e.matmul(
                    ps[:, 2560 + ((4 * g + j) % 8) * 16:
                       2560 + ((4 * g + j) % 8) * 16 + 9],
                    expT[:, gb * 512 + j * 128:gb * 512 + (j + 1) * 128],
                    vv, start=True, stop=True),
                    r=[f"expT{gb}", "vpad", "vz"], w=["OG"])
            if g % 2 == 1:
                t0 = 4 * (g - 1)
                p.op("act", lambda e, t0=t0: e.activation(
                    og_sb[:].rearrange("p (t n) -> p t n", t=8),
                    ps[:, 2560:2688].rearrange("p (t n) -> p t n", t=8)[:, :, 0:9],
                    AF.Copy), r=["OG"], w=["og_sb"])
                p.op("dve", lambda e, t0=t0, st=st: e.tensor_copy(
                    prod[:, (st * TT4 + t0) * 8:(st * TT4 + t0 + 8) * 8]
                    .rearrange("p (t n) -> p t n", t=8),
                    og_sb[:].rearrange("p (t n) -> p t n", t=8)[:, :, 0:8]),
                    r=["og_sb"], w=["prodraw"])
                p.op("dve", lambda e, t0=t0, st=st: e.tensor_copy(
                    rcp[:, t0:t0 + 8],
                    og_sb[:].rearrange("p (t n) -> p t n", t=8)[:, :, 8]),
                    r=["og_sb"], w=["rcpraw"])
        p.op("dve", lambda e: e.reciprocal(rcp[:], rcp[:]),
             r=["rcpraw"], w=["rcps"])
        p.op("dve", lambda e, st=st: e.tensor_tensor(
            prod[:, st * TT4 * 8:(st + 1) * TT4 * 8]
            .rearrange("p (t n) -> p t n", t=TT4),
            prod[:, st * TT4 * 8:(st + 1) * TT4 * 8]
            .rearrange("p (t n) -> p t n", t=TT4),
            rcp[:].unsqueeze(2).broadcast_to([128, TT4, 8]), ALU.mult),
            r=["prodraw", "rcps"], w=["prod"])

    xin_t = ctx.sbuf("xin_t", [128, 2 * 512], F32)

    def emit_pre(oc, th, dst, dres):
        hk = oc // 4
        xsl = xin_t[:, th * 512:(th + 1) * 512]
        p.dma("sp", lambda e, oc=oc, th=th, xsl=xsl: e.dma_start(
            out=xsl, in_=xin[oc * 128:(oc + 1) * 128, th * 512:(th + 1) * 512]),
            w=[f"xin_t{th}"])
        pv = prod[:].rearrange("p (t n) -> p t n", t=NT)[:, th * 512:(th + 1) * 512, hk]
        p.op("dve", lambda e, pv=pv, xsl=xsl, dst=dst: e.tensor_add(
            dst, xsl, pv), r=["prod", f"xin_t{th}"], w=[dres])

    fm_stage_ln(ctx, "cln", 512, 2, xout, emit_pre,
                mean_ps=ps[:, 0:1024], e2_ps=ps[:, 1024:2048],
                mres="PP0", eres="PP1")


def build_C():
    ctx = Ctx()
    ctx.load_consts()
    identd = ctx.din("identd", [128, 128], F32)
    ident = ctx.sbuf("identC", [128, 128], F32)
    ctx.p.dma("spd", lambda e: e.dma_start(out=ident[:], in_=identd), w=["ident"])
    pool = ctx.psum("pool", [128, 3584], F32)
    poolb = ctx.psum("poolb", [128, 1024], BF16)
    xin = ctx.din("xin", [DM, NT])
    xout = ctx.dout("xout", [DM, NT])
    emit_C(ctx, pool, poolb, xin, xout, ident)
    return ctx.build(["xout"])


def host_prep_C(inp):
    wq = np.asarray(inp["L1_attn_Wq"])[PI]
    wqp = wq.reshape(32, 128, 128, 32).transpose(1, 0, 3, 2)
    wqp = np.ascontiguousarray(wqp.reshape(128, 32 * 4096))
    wk = np.asarray(inp["L1_attn_Wk"])[PI]
    wkp = np.ascontiguousarray(
        wk.reshape(32, 128, 128, 8).transpose(1, 0, 3, 2).reshape(128, 32 * 1024))
    wv = np.asarray(inp["L1_attn_Wv"])[PI]
    wvp = np.ascontiguousarray(
        wv.reshape(32, 128, 128, 8).transpose(1, 0, 3, 2).reshape(128, 32 * 1024))
    return {"wqp": wqp, "wkp": wkp, "wvp": wvp,
            "bqp": np.ascontiguousarray(inp["L1_attn_bq"]),
            "bkp": np.ascontiguousarray(inp["L1_attn_bk"]),
            "bvp": np.ascontiguousarray(inp["L1_attn_bv"]),
            "identd": np.eye(128, dtype=np.float32),
            "consts_in": CONSTS_HOST}



def emit_A(ctx, pool, xbat, xown, ownq, xout):
    nc, p = ctx.nc, ctx.p
    LB = 4096
    winp = ctx.din("winp", [128, 32, 512])
    w1tp = ctx.din("w1tp", [4, 128, 4 * 2048])
    w2tp = ctx.din("w2tp", [4, 128, 16 * 512])
    woutp = ctx.din("woutp", [128, 4 * DM])
    ln1w = ctx.din("ln1w", [128, 16]);  ln1b = ctx.din("ln1b", [128, 16])
    ln2w = ctx.din("ln2w", [128, 16]);  ln2b = ctx.din("ln2b", [128, 16])
    sclc = ctx.din("sclc", [128, 16])
    phre = ctx.din("phre", [128, 16]);  phim = ctx.din("phim", [128, 16])
    lcre = ctx.din("lcre", [128, 16]);  lcim = ctx.din("lcim", [128, 16])
    binc = ctx.din("binc", [128, 4])
    b1c = ctx.din("b1c", [128, 64])
    b2r = ctx.din("b2r", [1, 2048])
    lnlw = ctx.din("lnlw", [128, 4]);  lnlb = ctx.din("lnlb", [128, 4])
    boutc = ctx.din("boutc", [128, 32])

    def ld(name, ap, shape, dt=F32):
        t = ctx.sbuf(name + "_sb", list(shape), dt)
        p.dma("sp", lambda e, t=t, ap=ap: e.dma_start(out=t[:], in_=ap),
              w=["cols"])
        return t
    ln1w_s = ld("ln1w", ln1w, [128, 16]); ln1b_s = ld("ln1b", ln1b, [128, 16])
    ln2w_s = ld("ln2w", ln2w, [128, 16]); ln2b_s = ld("ln2b", ln2b, [128, 16])
    scl_s = ld("sclc", sclc, [128, 16])
    phre_s = ld("phre", phre, [128, 16]); phim_s = ld("phim", phim, [128, 16])
    lcre_s = ld("lcre", lcre, [128, 16]); lcim_s = ld("lcim", lcim, [128, 16])
    binc_s = ld("binc", binc, [128, 4])
    b1c_s = ld("b1c", b1c, [128, 64])
    b2r_s = ld("b2r", b2r, [1, 2048])
    lnlw_s = ld("lnlw", lnlw, [128, 4]); lnlb_s = ld("lnlb", lnlb, [128, 4])
    boutc_s = ld("boutc", boutc, [128, 32])
    ownq_s = ld("ownq", ownq, [128, 4])

    rcol = ctx.sbuf("rcol", [128, 16], F32)
    thcol = ctx.sbuf("thcol", [128, 16], F32)
    tmpc = ctx.sbuf("tmpc", [128, 16], F32)
    tmpd = ctx.sbuf("tmpd", [128, 16], F32)
    p.op("act", lambda e: e.activation(rcol[:], phre_s[:], AF.Square),
         r=["cols"], w=["rcol"])
    p.op("act", lambda e: e.activation(tmpc[:], phim_s[:], AF.Square),
         r=["cols"], w=["tmpc"])
    p.op("dve", lambda e: e.tensor_add(rcol[:], rcol[:], tmpc[:]),
         r=["rcol", "tmpc"], w=["rcol"])
    p.op("act", lambda e: e.activation(rcol[:], rcol[:], AF.Sqrt),
         r=["rcol"], w=["rcol"])
    p.op("act", lambda e: e.activation(rcol[:], rcol[:], AF.Tanh),
         r=["rcol"], w=["rcol"])
    tmpe = ctx.sbuf("tmpe", [128, 16], F32)
    tmpf = ctx.sbuf("tmpf", [128, 16], F32)
    p.op("dve", lambda e: e.reciprocal(tmpf[:], phre_s[:]),
         r=["cols"], w=["tmpf"])
    p.op("dve", lambda e: e.tensor_mul(thcol[:], phim_s[:], tmpf[:]),
         r=["cols", "tmpf"], w=["thcol"])
    p.op("act", lambda e: e.activation(tmpe[:], thcol[:], AF.Sign),
         r=["thcol"], w=["tmpe"])
    p.op("act", lambda e: e.activation(thcol[:], thcol[:], AF.Abs),
         r=["thcol"], w=["thcol"])
    p.op("dve", lambda e: e.tensor_scalar(tmpf[:], thcol[:], 1e-30, None,
                                          ALU.max), r=["thcol"], w=["tmpf"])
    p.op("dve", lambda e: e.reciprocal(tmpf[:], tmpf[:]),
         r=["tmpf"], w=["tmpf"])
    p.op("dve", lambda e: e.tensor_scalar(tmpd[:], thcol[:], 1.0, None,
                                          ALU.is_gt), r=["thcol"], w=["tmpd"])
    p.op("dve", lambda e: e.tensor_tensor(thcol[:], thcol[:], tmpf[:],
                                          ALU.min), r=["thcol", "tmpf"],
         w=["thcol"])
    p.op("act", lambda e: e.activation(thcol[:], thcol[:], AF.Arctan),
         r=["thcol"], w=["thcol"])
    p.op("dve", lambda e: e.scalar_tensor_tensor(
        tmpf[:], thcol[:], -2.0, tmpd[:], ALU.mult, ALU.bypass),
        r=["thcol", "tmpd"], w=["tmpf"]) if False else None
    p.op("dve", lambda e: e.tensor_scalar(tmpf[:], thcol[:], -2.0, PIHALF,
                                          ALU.mult, ALU.add),
         r=["thcol", "consts"], w=["tmpf"])
    p.op("dve", lambda e: e.tensor_mul(tmpf[:], tmpf[:], tmpd[:]),
         r=["tmpf", "tmpd"], w=["tmpf"])
    p.op("dve", lambda e: e.tensor_add(thcol[:], thcol[:], tmpf[:]),
         r=["thcol", "tmpf"], w=["thcol"])
    p.op("dve", lambda e: e.tensor_mul(thcol[:], thcol[:], tmpe[:]),
         r=["thcol", "tmpe"], w=["thcol"])
    p.op("act", lambda e: e.activation(tmpc[:], phim_s[:], AF.Sign),
         r=["cols"], w=["tmpc"])
    p.op("dve", lambda e: e.tensor_scalar(tmpd[:], phre_s[:], 0.0, None,
                                          ALU.is_lt), r=["cols"], w=["tmpd"])
    p.op("dve", lambda e: e.tensor_mul(tmpc[:], tmpc[:], tmpd[:]),
         r=["tmpc", "tmpd"], w=["tmpc"])
    p.op("dve", lambda e: e.scalar_tensor_tensor(
        thcol[:], tmpc[:], float(np.pi), thcol[:], ALU.mult, ALU.add),
        r=["tmpc", "thcol", "consts"], w=["thcol"])

    ps = pool
    iot = ctx.sbuf("iot", [128, LB], F32)
    p.op("pool", lambda e: e.iota(iot[:], pattern=[[1, LB]], base=1,
                                  channel_multiplier=0,
                                  allow_small_or_imprecise_dtypes=True),
         w=["iot"])
    ones512 = ctx.sbuf("ones512", [1, 512], F32)
    p.op("dve", lambda e: e.memset(ones512[:], 1.0), w=["ones512"])
    ones_bf = ctx.sbuf("onesA", [128, 128], BF16)
    p.op("dve", lambda e: e.memset(ones_bf[:], 1.0), w=["onesA"])

    h = ctx.sbuf("h", [128, 4 * LB], F32)
    ybf = ctx.sbuf("ybf", [128, 4 * LB], BF16)
    WB = ctx.sbuf("WBb", [128, 16384], BF16)
    fbuf = ctx.sbuf("fbuf", [128, 8 * 512], BF16)
    TA = ctx.sbuf("TA", [128, 1024], F32)
    TB = ctx.sbuf("TB", [128, 1024], F32)
    WR = ctx.sbuf("WR", [128, 1024], F32)
    UR = ctx.sbuf("UR", [128, 1024], F32)
    CBt = ctx.sbuf("CBt", [128, 1024], BF16)
    SBt = ctx.sbuf("SBt", [128, 1024], BF16)
    ucar = ctx.sbuf("ucar", [128, 2], F32)
    hbf4 = ctx.sbuf("hbf4", [128, 2048], BF16)
    sqb4 = ctx.sbuf("sqb4", [128, 2048], BF16)
    mu4 = ctx.sbuf("mu4", [128, 512], F32)
    rs4 = ctx.sbuf("rs4", [128, 512], F32)
    wtl = ctx.sbuf("wtl", [128, 2 * 512], BF16)
    xtl = ctx.sbuf("xtl", [128, 2 * 512], BF16)

    for tq in range(8):
        tsl = slice(tq * 512, (tq + 1) * 512)
        for ic in range(32):
            bi = ic % 2
            p.dma("spd", lambda e, ic=ic, bi=bi: e.dma_start(
                out=wtl[:, bi * 512:(bi + 1) * 512],
                in_=winp[:, ic, :]), r=[f"wtl{bi}"], w=[f"wtl{bi}"])
            p.dma("spd", lambda e, ic=ic, bi=bi, tsl=tsl: e.dma_start(
                out=xtl[:, bi * 512:(bi + 1) * 512],
                in_=xbat[ic * 128:(ic + 1) * 128, tsl]),
                r=[f"xtl{bi}"], w=[f"xtl{bi}"])
            for dd in range(4):
                p.op("pe", lambda e, ic=ic, bi=bi, dd=dd: e.matmul(
                    ps[:, 1024 + dd * 512:1024 + (dd + 1) * 512],
                    wtl[:, bi * 512 + dd * 128:bi * 512 + (dd + 1) * 128],
                    xtl[:, bi * 512:(bi + 1) * 512],
                    start=(ic == 0), stop=(ic == 31)),
                    r=[f"wtl{bi}", f"xtl{bi}"], w=[f"HB{dd}"])
        for dd in range(4):
            p.op("dve", lambda e, dd=dd, tq=tq: e.tensor_scalar(
                h[:, dd * LB + tq * 512:dd * LB + (tq + 1) * 512],
                ps[:, 1024 + dd * 512:1024 + (dd + 1) * 512],
                binc_s[:, dd:dd + 1], None, ALU.add),
                r=[f"HB{dd}", "cols"], w=["h"])

    def ln512(wcol_s, bcol_s, lay, silu, src_res="h"):
        for tc in range(8):
            tsl = slice(tc * 512, (tc + 1) * 512)
            for ch in range(4):
                p.op("dve", lambda e, ch=ch, tsl=tsl: e.tensor_copy(
                    hbf4[:, ch * 512:(ch + 1) * 512], h[:, ch * LB:][:, tsl]),
                    r=["h"], w=["hbf4"])
            p.op("act", lambda e: e.activation(sqb4[:], hbf4[:], AF.Square),
                 r=["hbf4"], w=["sqb4"])
            for ch in range(4):
                p.op("pe", lambda e, ch=ch: e.matmul(
                    ps[:, 0:512], ones_bf[:], hbf4[:, ch * 512:(ch + 1) * 512],
                    start=(ch == 0), stop=(ch == 3)),
                    r=["hbf4", "onesA"], w=["B0"])
                p.op("pe", lambda e, ch=ch: e.matmul(
                    ps[:, 512:1024], ones_bf[:], sqb4[:, ch * 512:(ch + 1) * 512],
                    start=(ch == 0), stop=(ch == 3)),
                    r=["sqb4", "onesA"], w=["B1"])
            p.op("act", lambda e: e.activation(mu4[:], ps[:, 0:512], AF.Copy,
                                               scale=1.0 / 512),
                 r=["B0"], w=["mu4"])
            p.op("act", lambda e: e.activation(rs4[:], ps[:, 512:1024], AF.Copy,
                                               scale=1.0 / 512),
                 r=["B1"], w=["rs4"])
            p.op("dve", lambda e: e.tensor_mul(TA[:, 0:512], mu4[:], mu4[:]),
                 r=["mu4"], w=["TA"])
            p.op("dve", lambda e: e.tensor_sub(rs4[:], rs4[:], TA[:, 0:512]),
                 r=["rs4", "TA"], w=["rs4"])
            p.op("act", lambda e: e.activation(rs4[:], rs4[:], AF.Ln, bias=EPS),
                 r=["rs4", "consts"], w=["rs4"])
            p.op("act", lambda e: e.activation(rs4[:], rs4[:], AF.Exp, scale=-0.5),
                 r=["rs4"], w=["rs4"])
            for ch in range(4):
                dst = TB[:, 0:512]
                p.op("dve", lambda e, ch=ch, tsl=tsl: e.tensor_sub(
                    TB[:, 0:512], h[:, ch * LB:][:, tsl], mu4[:]),
                    r=["h", "mu4", "TB"], w=["TB"])
                p.op("dve", lambda e: e.tensor_mul(TB[:, 0:512], TB[:, 0:512],
                                                   rs4[:]),
                     r=["TB", "rs4"], w=["TB"])
                p.op("dve", lambda e, ch=ch, lay=lay, wcol_s=wcol_s,
                     bcol_s=bcol_s: e.tensor_scalar(
                         TB[:, 0:512], TB[:, 0:512],
                         wcol_s[:, lay * 4 + ch:lay * 4 + ch + 1],
                         bcol_s[:, lay * 4 + ch:lay * 4 + ch + 1],
                         ALU.mult, ALU.add),
                     r=["TB", "cols"], w=["TB"])
                if silu:
                    p.op("act", lambda e, ch=ch, tsl=tsl: e.activation(
                        ybf[:, ch * LB:][:, tsl], TB[:, 0:512], AF.Silu),
                        r=["TB"], w=["ybf"])
                else:
                    p.op("dve", lambda e, ch=ch, tsl=tsl: e.tensor_copy(
                        ybf[:, ch * LB:][:, tsl], TB[:, 0:512]),
                        r=["TB"], w=["ybf"])

    for lay in range(4):
        ln512(ln1w_s, ln1b_s, lay, True)
        for ch in range(4):
            s16 = lay * 4 + ch
            for qq in range(4):
                lsl = slice(qq * 1024, (qq + 1) * 1024)
                p.op("dve", lambda e, qq=qq, s16=s16, lsl=lsl: e.tensor_scalar(
                    TA[:], iot[:, lsl], thcol[:, s16:s16 + 1], None, ALU.mult),
                    r=["iot", "thcol", "TA"], w=["TA"])
                p.op("act", lambda e: e.activation(TB[:], TA[:], AF.Copy,
                                                   scale=INV_2PI),
                     r=["TA"], w=["TB"])
                p.op("act", lambda e: e.activation(TB[:], TB[:], AF.Copy,
                                                   bias=MAGIC), r=["TB", "consts"],
                     w=["TB"])
                p.op("act", lambda e: e.activation(TB[:], TB[:], AF.Copy,
                                                   bias=-MAGIC), r=["TB", "consts"],
                     w=["TB"])
                p.op("dve", lambda e: e.scalar_tensor_tensor(
                    TA[:], TB[:], -TWO_PI, TA[:], ALU.mult, ALU.add),
                    r=["TB", "TA", "consts"], w=["TA"])
                p.op("dve", lambda e: e.tensor_scalar(
                    TA[:], TA[:], float(np.pi), -float(np.pi),
                    ALU.min, ALU.max), r=["TA", "consts"], w=["TA"])
                p.op("act", lambda e: e.activation(TB[:], TA[:], AF.Abs),
                     r=["TA"], w=["TB"])
                p.op("act", lambda e: e.activation(CBt[:], TB[:], AF.Sin,
                                                   scale=-1.0, bias=PIHALF),
                     r=["TB", "consts"], w=["CBt"])
                p.op("act", lambda e: e.activation(SBt[:], TA[:], AF.Sin,
                                                   scale=-1.0),
                     r=["TA"], w=["SBt"])
                p.op("dve", lambda e, s16=s16: e.tensor_scalar(
                    TB[:], TB[:], 0.0, rcol[:, s16:s16 + 1], ALU.mult, ALU.add),
                    r=["TB", "rcol"], w=["TB"])
                p.op("dve", lambda e, ch=ch, lsl=lsl: e.tensor_mul(
                    WR[:], ybf[:, ch * LB:][:, lsl], CBt[:]),
                    r=["ybf", "CBt"], w=["WR"])
                if qq == 0:
                    ini_r = lcre_s[:, s16:s16 + 1]
                    ini_i = lcim_s[:, s16:s16 + 1]
                    rr = ["cols"]
                else:
                    ini_r = ucar[:, 0:1]
                    ini_i = ucar[:, 1:2]
                    rr = ["ucar"]
                p.op("dve", lambda e, ini_r=ini_r: e.tensor_tensor_scan(
                    TA[:], TB[:], WR[:], ini_r, ALU.mult, ALU.add),
                    r=["TB", "WR", "TA"] + rr, w=["TA"])
                p.op("dve", lambda e: e.tensor_copy(ucar[:, 0:1], TA[:, 1023:1024]),
                     r=["TA"], w=["ucar"])
                p.op("dve", lambda e, ch=ch, lsl=lsl: e.tensor_mul(
                    WR[:], ybf[:, ch * LB:][:, lsl], SBt[:]),
                    r=["ybf", "SBt", "WR"], w=["WR"])
                p.op("dve", lambda e, ini_i=ini_i: e.tensor_tensor_scan(
                    UR[:], TB[:], WR[:], ini_i, ALU.mult, ALU.add),
                    r=["TB", "WR"] + rr, w=["UR"])
                p.op("dve", lambda e: e.tensor_copy(ucar[:, 1:2], UR[:, 1023:1024]),
                     r=["UR"], w=["ucar"])
                p.op("dve", lambda e: e.tensor_mul(WR[:], TA[:], CBt[:]),
                     r=["TA", "CBt"], w=["WR"])
                p.op("dve", lambda e: e.tensor_mul(TB[:], UR[:], SBt[:]),
                     r=["UR", "SBt", "TB"], w=["TB"])
                p.op("dve", lambda e: e.tensor_add(WR[:], WR[:], TB[:]),
                     r=["WR", "TB"], w=["WR"])
                p.op("dve", lambda e, ch=ch, lsl=lsl, s16=s16: e.scalar_tensor_tensor(
                    h[:, ch * LB:][:, lsl], WR[:], scl_s[:, s16:s16 + 1],
                    h[:, ch * LB:][:, lsl], ALU.mult, ALU.add),
                    r=["WR", "cols", "h"], w=["h"])
        ln512(ln2w_s, ln2b_s, lay, False)
        p.dma("spd", lambda e, lay=lay: e.dma_start(
            out=WB[:, 0:8192], in_=w1tp[lay]), r=["WBb"], w=["WBb"])
        p.dma("spd", lambda e, lay=lay: e.dma_start(
            out=WB[:, 8192:16384], in_=w2tp[lay]), r=["WBb"], w=["WBb"])
        for tc in range(8):
            tsl = slice(tc * 512, (tc + 1) * 512)
            for ph in range(2):
                for oc8 in range(8):
                    oc = ph * 8 + oc8
                    bb = oc % 2
                    for kc in range(4):
                        p.op("pe", lambda e, oc=oc, kc=kc, bb=bb, tsl=tsl: e.matmul(
                            ps[:, bb * 512:(bb + 1) * 512],
                            WB[:, kc * 2048 + oc * 128:kc * 2048 + (oc + 1) * 128],
                            ybf[:, kc * LB:][:, tsl],
                            start=(kc == 0), stop=(kc == 3)),
                            r=["WBb", "ybf"], w=[f"FB{bb}"])
                    p.op("act", lambda e, oc=oc, oc8=oc8, bb=bb, lay=lay:
                         e.activation(
                             fbuf[:, oc8 * 512:(oc8 + 1) * 512],
                             ps[:, bb * 512:(bb + 1) * 512], AF.Silu,
                             bias=b1c_s[:, lay * 16 + oc:lay * 16 + oc + 1]),
                         r=[f"FB{bb}", "cols"], w=["fbuf"])
                for dd in range(4):
                    for kc8 in range(8):
                        kc = ph * 8 + kc8
                        p.op("pe", lambda e, dd=dd, kc=kc, kc8=kc8, ph=ph:
                             e.matmul(
                                 ps[:, 1024 + dd * 512:1024 + (dd + 1) * 512],
                                 WB[:, 8192 + kc * 512 + dd * 128:
                                    8192 + kc * 512 + (dd + 1) * 128],
                                 fbuf[:, kc8 * 512:(kc8 + 1) * 512],
                                 start=(ph == 0 and kc8 == 0), stop=False),
                                 r=["WBb", "fbuf"], w=[f"HB{dd}"])
            for dd in range(4):
                p.op("pe", lambda e, dd=dd, lay=lay: e.matmul(
                    ps[:, 1024 + dd * 512:1024 + (dd + 1) * 512],
                    b2r_s[:, (lay * 4 + dd) * 128:(lay * 4 + dd + 1) * 128],
                    ones512[:], start=False, stop=True),
                    r=["cols", "ones512"], w=[f"HB{dd}"])
                p.op("dve", lambda e, dd=dd, tsl=tsl: e.scalar_tensor_tensor(
                    h[:, dd * LB:][:, tsl],
                    ps[:, 1024 + dd * 512:1024 + (dd + 1) * 512], 1.0,
                    h[:, dd * LB:][:, tsl], ALU.mult, ALU.add),
                    r=[f"HB{dd}", "h"], w=["h"])

    def hosl(ch, tsl=None):
        base = h[:, ch * LB:ch * LB + NT]
        return base if tsl is None else h[:, ch * LB:ch * LB + NT][:, tsl]
    for ch in range(4):
        p.op("dve", lambda e, ch=ch: e.tensor_scalar(
            hosl(ch), h[:, ch * LB:ch * LB + NT],
            ownq_s[:, 0:1], None, ALU.mult), r=["h", "cols"], w=["h", "ho"])
        for q in range(1, 4):
            p.op("dve", lambda e, ch=ch, q=q: e.scalar_tensor_tensor(
                hosl(ch), h[:, ch * LB + q * NT:ch * LB + (q + 1) * NT],
                ownq_s[:, q:q + 1], hosl(ch),
                ALU.mult, ALU.add), r=["h", "cols", "ho"], w=["h", "ho"])

    for tc in range(2):
        tsl = slice(tc * 512, (tc + 1) * 512)
        for ch in range(4):
            p.op("dve", lambda e, ch=ch, tsl=tsl: e.tensor_copy(
                hbf4[:, ch * 512:(ch + 1) * 512], hosl(ch, tsl)),
                r=["ho"], w=["hbf4"])
        p.op("act", lambda e: e.activation(sqb4[:], hbf4[:], AF.Square),
             r=["hbf4"], w=["sqb4"])
        for ch in range(4):
            p.op("pe", lambda e, ch=ch: e.matmul(
                ps[:, 0:512], ones_bf[:], hbf4[:, ch * 512:(ch + 1) * 512],
                start=(ch == 0), stop=(ch == 3)), r=["hbf4", "onesA"], w=["B0"])
            p.op("pe", lambda e, ch=ch: e.matmul(
                ps[:, 512:1024], ones_bf[:], sqb4[:, ch * 512:(ch + 1) * 512],
                start=(ch == 0), stop=(ch == 3)), r=["sqb4", "onesA"], w=["B1"])
        p.op("act", lambda e: e.activation(mu4[:], ps[:, 0:512], AF.Copy,
                                           scale=1.0 / 512), r=["B0"], w=["mu4"])
        p.op("act", lambda e: e.activation(rs4[:], ps[:, 512:1024], AF.Copy,
                                           scale=1.0 / 512), r=["B1"], w=["rs4"])
        p.op("dve", lambda e: e.tensor_mul(TA[:, 0:512], mu4[:], mu4[:]),
             r=["mu4"], w=["TA"])
        p.op("dve", lambda e: e.tensor_sub(rs4[:], rs4[:], TA[:, 0:512]),
             r=["rs4", "TA"], w=["rs4"])
        p.op("act", lambda e: e.activation(rs4[:], rs4[:], AF.Ln, bias=EPS),
             r=["rs4", "consts"], w=["rs4"])
        p.op("act", lambda e: e.activation(rs4[:], rs4[:], AF.Exp, scale=-0.5),
             r=["rs4"], w=["rs4"])
        for ch in range(4):
            p.op("dve", lambda e, ch=ch, tsl=tsl: e.tensor_sub(
                TB[:, 0:512], hosl(ch, tsl), mu4[:]),
                r=["ho", "mu4", "TB"], w=["TB"])
            p.op("dve", lambda e: e.tensor_mul(TB[:, 0:512], TB[:, 0:512], rs4[:]),
                 r=["TB", "rs4"], w=["TB"])
            p.op("dve", lambda e, ch=ch, tsl=tsl: e.tensor_scalar(
                ybf[:, ch * NT:][:, tsl], TB[:, 0:512],
                lnlw_s[:, ch:ch + 1], lnlb_s[:, ch:ch + 1], ALU.mult, ALU.add),
                r=["TB", "cols"], w=["ybf"])

    p.dma("spd", lambda e: e.dma_start(out=WB[:], in_=woutp), r=["WBb"], w=["WBb"])
    xin_t = ctx.sbuf("xin_tA", [128, 2 * 512], F32)

    def emit_pre(oc, th, dst, dres):
        bb = th
        psl = ps[:, 1024 + bb * 512:1024 + (bb + 1) * 512]
        for kc in range(4):
            p.op("pe", lambda e, oc=oc, th=th, kc=kc, psl=psl: e.matmul(
                psl, WB[:, kc * DM + oc * 128:kc * DM + (oc + 1) * 128],
                ybf[:, kc * NT + th * 512:kc * NT + (th + 1) * 512],
                start=(kc == 0), stop=(kc == 3)),
                r=["WBb", "ybf"], w=[f"HB{bb}"])
        xsl = xin_t[:, th * 512:(th + 1) * 512]
        p.dma("sp", lambda e, oc=oc, th=th, xsl=xsl: e.dma_start(
            out=xsl, in_=xown[oc * 128:(oc + 1) * 128, th * 512:(th + 1) * 512]),
            w=[f"xinA{th}"])
        p.op("dve", lambda e, psl=psl, xsl=xsl, dst=dst: e.scalar_tensor_tensor(
            dst, psl, 1.0, xsl, ALU.mult, ALU.add),
            r=[f"HB{bb}", f"xinA{th}"], w=[dres])
        p.op("dve", lambda e, oc=oc, dst=dst: e.tensor_scalar(
            dst, dst, boutc_s[:, oc:oc + 1], None, ALU.add),
            r=["cols", dres], w=[dres])

    fm_stage_ln(ctx, "aln", 512, 2, xout, emit_pre,
                mean_ps=ps[:, 0:1024], e2_ps=ps[:, 2048:3072],
                mres="B0", eres="HB2e",
                bufs=(TA, CBt, SBt, WR, UR, TB))


def build_A():
    ctx = Ctx()
    ctx.load_consts()
    pool = ctx.psum("pool", [128, 3584], F32)
    xbat = ctx.din("xbat", [DM, 4096])
    xown = ctx.din("xown", [DM, NT])
    ownq = ctx.din("ownq", [128, 4])
    xout = ctx.dout("xout", [DM, NT])
    emit_A(ctx, pool, xbat, xown, ownq, xout)
    return ctx.build(["xout"])


def host_prep_A(inp):
    def cols16(a):
        return np.ascontiguousarray(
            np.asarray(a).reshape(4, 4, 128).transpose(2, 0, 1).reshape(128, 16))
    WinT = np.asarray(inp["L0_attn_Win"]).T[PI]
    winp = np.ascontiguousarray(WinT.reshape(32, 128, 512).transpose(1, 0, 2))
    w1tp = np.ascontiguousarray(
        np.asarray(inp["L0_attn_W1"]).transpose(0, 2, 1)
        .reshape(4, 4, 128, 2048).transpose(0, 2, 1, 3).reshape(4, 128, 4 * 2048))
    w2tp = np.ascontiguousarray(
        np.asarray(inp["L0_attn_W2"]).transpose(0, 2, 1)
        .reshape(4, 16, 128, 512).transpose(0, 2, 1, 3).reshape(4, 128, 16 * 512))
    woutp = np.ascontiguousarray(
        np.asarray(inp["L0_attn_Wout"])[PI].T.reshape(4, 128, DM)
        .transpose(1, 0, 2).reshape(128, 4 * DM))
    d = {
        "winp": winp, "w1tp": w1tp, "w2tp": w2tp, "woutp": woutp,
        "ln1w": cols16(inp["L0_attn_ln1_w"]), "ln1b": cols16(inp["L0_attn_ln1_b"]),
        "ln2w": cols16(inp["L0_attn_ln2_w"]), "ln2b": cols16(inp["L0_attn_ln2_b"]),
        "sclc": cols16(inp["L0_attn_sc_lin"]),
        "phre": cols16(inp["L0_attn_phz_re"]), "phim": cols16(inp["L0_attn_phz_im"]),
        "lcre": cols16(inp["L0_attn_lc_re"]), "lcim": cols16(inp["L0_attn_lc_im"]),
        "binc": np.ascontiguousarray(
            np.asarray(inp["L0_attn_bin"]).reshape(4, 128).T),
        "b1c": np.ascontiguousarray(
            np.asarray(inp["L0_attn_b1"]).reshape(4, 16, 128)
            .transpose(2, 0, 1).reshape(128, 64)),
        "b2r": np.ascontiguousarray(
            np.asarray(inp["L0_attn_b2"]).reshape(1, 2048)),
        "lnlw": np.ascontiguousarray(
            np.asarray(inp["L0_attn_lnl_w"]).reshape(4, 128).T),
        "lnlb": np.ascontiguousarray(
            np.asarray(inp["L0_attn_lnl_b"]).reshape(4, 128).T),
        "boutc": np.ascontiguousarray(
            np.asarray(inp["L0_attn_bout"])[PI].reshape(32, 128).T),
        "identd": np.eye(128, dtype=np.float32),
        "consts_in": CONSTS_HOST,
    }
    return d



def build_fused():
    ctx = Ctx()
    p = ctx.p
    ctx.load_consts()
    identd = ctx.din("identd", [128, 128], F32)
    ident = ctx.sbuf("identG", [128, 128], F32)
    p.dma("spd", lambda e: e.dma_start(out=ident[:], in_=identd), w=["ident"])
    pool = ctx.psum("pool", [128, 3584], F32)
    poolb = ctx.psum("poolb", [128, 1024], BF16)

    xbat = ctx.din("xbat", [DM, 4096])
    xown = ctx.din("xown", [DM, NT])
    ownq = ctx.din("ownq", [128, 4])
    out_tm = ctx.dout("out_tm", [NT, DM])
    x1d = ctx.dram("x1d", [DM, NT])
    x2d = ctx.dram("x2d", [DM, NT])
    x3d = ctx.dram("x3d", [DM, NT])

    with ctx.scope() as sc:
        ctx.cur_scope = sc
        emit_A(ctx, pool, xbat, xown, ownq, x1d)
        ctx.cur_scope = None
    p.barrier()
    with ctx.scope() as sc:
        ctx.cur_scope = sc
        emit_B(ctx, pool, x1d, x2d, ident)
        ctx.cur_scope = None
    p.barrier()
    with ctx.scope() as sc:
        ctx.cur_scope = sc
        emit_C(ctx, pool, poolb, x2d, x3d, ident)
        ctx.cur_scope = None
    p.barrier()
    with ctx.scope() as sc:
        ctx.cur_scope = sc
        emit_D(ctx, pool, x3d, out_tm, ident=ident)
        ctx.cur_scope = None
    return ctx.build(["xout"])


def input_names(nc):
    out = set()
    for alloc in nc.m.functions[0].allocations:
        try:
            if alloc.kind == "ExternalInput":
                out.add(alloc.memorylocations[0].name)
        except AttributeError:
            pass
    return out


_NC_CACHE = {}


def _get_nc(name):
    if name not in _NC_CACHE:
        _NC_CACHE[name] = {"A": build_A, "B": build_B, "C": build_C,
                           "D": build_D, "F": build_fused}[name]()
    return _NC_CACHE[name]


def kernel(**inp):
    x = np.asarray(inp["L0_attn_x"], dtype=np.float32).reshape(8192, DM)
    packs = {}
    packs.update(host_prep_A(inp))
    packs.update(host_prep_B(inp))
    packs.update(host_prep_C(inp))
    packs.update(host_prep_D(inp))
    packs["identd"] = np.eye(128, dtype=np.float32)
    packs["consts_in"] = CONSTS_HOST

    ncF = _get_nc("F")
    names = input_names(ncF)
    xbat = [np.ascontiguousarray(x[b * 4096:(b + 1) * 4096][:, PI].T)
            for b in range(2)]
    in_maps = []
    for c in range(8):
        q = c % 4
        ownq = np.zeros((128, 4), np.float32)
        ownq[:, q] = 1.0
        m = {"xbat": xbat[c // 4],
             "xown": np.ascontiguousarray(xbat[c // 4][:, q * NT:(q + 1) * NT]),
             "ownq": ownq}
        m.update({k: v for k, v in packs.items() if k in names})
        assert set(m) >= (names - {"partition_id"}), sorted(names - set(m))
        in_maps.append(m)
    res = run_bass_kernel_spmd(ncF, in_maps, core_ids=list(range(8)))

    out = np.empty((8192, DM), dtype=np.float32)
    for c in range(8):
        t0 = (c // 4) * 4096 + (c % 4) * NT
        out[t0:t0 + NT] = np.asarray(res.results[c]["out_tm"])[:, PIINV]
    return out.reshape(2, 4096, DM)
